# version 07
# baseline (speedup 1.0000x reference)
import numpy as np
import ml_dtypes
import concourse.bass as bass
import concourse.mybir as mybir
from concourse.bass_utils import run_bass_kernel_spmd
from contextlib import ExitStack

F32 = mybir.dt.float32
BF16 = mybir.dt.bfloat16
AF = mybir.ActivationFunctionType
ALU = mybir.AluOpType
AX = mybir.AxisListType

NCORES = 8


class Cfg:
    def __init__(self, D=1024, CI=2048, NS=16, RDT=64, L=2048, BL=4, E=8,
                 HID=4096, FF_G=1024, TA=512, TB=256, ftscap=512, fdcap=512):
        self.ftscap, self.fdcap = ftscap, fdcap
        self.D, self.CI, self.NS, self.RDT = D, CI, NS, RDT
        self.L, self.BL, self.E, self.HID = L, BL, E, HID
        self.FF_G, self.TA, self.TB = FF_G, TA, TB
        self.C2 = 2 * CI
        self.T = BL * L
        assert self.T % FF_G == 0 and FF_G % 128 == 0
        assert L % TA == 0 and L % TB == 0 and TA % 128 == 0 and TB % 128 == 0
        assert D % 128 == 0 and CI % 128 == 0 and HID % 128 == 0
        assert HID % 4 == 0 and (HID // 4) % 128 == 0
        assert RDT <= 128 and NS <= 16 and TA <= 512 and TB <= 512


FULL = Cfg()
ENGINES = ("sync", "scalar", "vector", "tensor", "gpsimd")


class Counter:
    __slots__ = ("h", "n")

    def __init__(self, h):
        self.h = h
        self.n = 0


class Prog:
    ENG2SEM = {"tensor": "PE", "scalar": "ACT", "vector": "DVE",
               "gpsimd": "PL", "sync": "SY"}

    def __init__(self, nc):
        self.nc = nc
        self.items = []

    def init_sems(self, ctx):
        e = ctx.enter_context
        self._ctx = ctx
        self.sems = {}
        self.streams = {}
        for name in ("PE", "ACT", "DVE", "PL", "SY"):
            self.sems[name] = Counter(e(self.nc.semaphore(name)))

    def on(self, eng, fn):
        self.items.append((eng, fn))

    def stream(self, name):
        if name not in self.streams:
            self.streams[name] = Counter(
                self._ctx.enter_context(self.nc.semaphore("sm_" + name)))
        return self.streams[name]

    def dma(self, queue, stream, out, in_, **kw):
        sem = self.stream(stream)
        sem.n += 16
        v = sem.n

        def fn(e, out=out, in_=in_, kw=kw, sem=sem):
            e.dma_start(out=out, in_=in_, **kw).then_inc(sem.h, 16)

        self.on(queue, fn)
        return v

    def dwait(self, eng, stream, val=None):
        if stream not in self.streams:
            return
        sem = self.streams[stream]
        v = sem.n if val is None else val
        if v == 0:
            return

        def fn(e, sem=sem, v=v):
            e.wait_ge(sem.h, v)

        self.on(eng, fn)

    def wait(self, eng, sem_name, val):
        if val is None:
            return
        sem = self.sems[sem_name]

        def fn(e, sem=sem, val=val):
            e.wait_ge(sem.h, val)

        self.on(eng, fn)

    def run(self, eng, fn, inc=False, drain=False):
        sem = self.sems[self.ENG2SEM[eng]]
        v = None
        if inc or drain:
            sem.n += 1
            v = sem.n
        if inc:
            def wrapped(e, fn=fn, sem=sem):
                fn(e).then_inc(sem.h, 1)
            self.on(eng, wrapped)
        elif drain:
            def wrapped(e, fn=fn, sem=sem):
                fn(e)
                e.drain().then_inc(sem.h, 1)
            self.on(eng, wrapped)
        else:
            self.on(eng, fn)
        return v

    def drain_inc(self, eng):
        sem = self.sems[self.ENG2SEM[eng]]
        sem.n += 1
        v = sem.n

        def fn(e, sem=sem):
            e.drain().then_inc(sem.h, 1)

        self.on(eng, fn)
        return v

    def barrier(self, engines=ENGINES):
        for s in list(self.streams):
            self.dwait("sync", s)
        vals = {}
        for eng in engines:
            sem = self.sems[self.ENG2SEM[eng]]
            nprev = sem.n
            sem.n += 1
            vals[eng] = sem.n
            if eng in ("vector", "scalar", "gpsimd"):
                def fn(e, sem=sem):
                    e.drain().then_inc(sem.h, 1)
            else:
                def fn(e, sem=sem, nprev=nprev):
                    if nprev:
                        e.wait_ge(sem.h, nprev)
                    e.nop().then_inc(sem.h, 1)
            self.on(eng, fn)
        for eng in engines:
            for other in engines:
                if other != eng:
                    self.wait(eng, self.ENG2SEM[other], vals[other])

    def emit_all(self):
        nc = self.nc
        with nc.Block() as block:
            for eng_name in ENGINES:
                def body(e, eng_name=eng_name):
                    for eng, fn in self.items:
                        if eng == eng_name:
                            fn(e)
                getattr(block, eng_name)(body)


def build_program(cfg: Cfg, debug_outs=False):
    nc = bass.Bass("TRN2", target_bir_lowering=False)
    c = cfg
    KD = c.D // 128
    KC = c.CI // 128
    KH = c.HID // 128
    TPG = c.FF_G // 128
    NG = c.T // c.FF_G
    NCA = c.L // c.TA
    NCB = c.L // c.TB
    LT = c.L // 128
    FTS = min(c.ftscap, c.FF_G)
    NTS = c.FF_G // FTS
    FD = min(c.fdcap, c.D)
    NF = c.D // FD
    HQ = c.HID // 4
    NHT = HQ // 128
    NXP = c.RDT + 2 * c.NS

    def din(name, shape, dt=F32):
        return nc.dram_tensor(name, list(shape), dt, kind="ExternalInput")

    x_d = din("x", (c.T, c.D))
    fwgT = din("fwgT", (c.E, c.D, c.HID))
    fwuT = din("fwuT", (c.E, c.D, c.HID))
    fwdT = din("fwdT", (c.E, c.HID, c.D))
    fgT = din("fgT", (c.D, c.E))
    inwT = [din("inwT1", (c.D, c.C2)), din("inwT2", (c.D, c.C2))]
    outwT = [din("outwT1", (c.CI, c.D)), din("outwT2", (c.CI, c.D))]
    xpwT = [din("xpwT1", (c.CI, NXP)), din("xpwT2", (c.CI, NXP))]
    dtpwT = [din("dtpwT1", (c.RDT, c.CI)), din("dtpwT2", (c.RDT, c.CI))]
    convw = [din("convw1", (128, KC * 4)), din("convw2", (128, KC * 4))]
    convb = [din("convb1", (128, KC)), din("convb2", (128, KC))]
    dtpb = [din("dtpb1", (128, KC)), din("dtpb2", (128, KC))]
    dvec = [din("dvec1", (128, KC)), din("dvec2", (128, KC))]
    alog = [din("alog1", (128, KC * c.NS)), din("alog2", (128, KC * c.NS))]
    ident_d = din("ident", (128, 128), BF16)
    antid_d = din("antid", (128, 128), BF16)
    out_d = nc.dram_tensor("out", [c.T, c.D], F32, kind="ExternalOutput")

    def scr(name, shape, dt):
        return nc.dram_tensor(name, list(shape), dt, kind="Internal")

    bwg = scr("bwg", (c.E, c.D, c.HID), BF16)
    bwu = scr("bwu", (c.E, c.D, c.HID), BF16)
    bwd = scr("bwd", (c.E, c.HID, c.D), BF16)
    bgT = scr("bgT", (c.D, c.E), BF16)
    binw = [scr("binw1", (c.D, c.C2), BF16), scr("binw2", (c.D, c.C2), BF16)]
    boutw = [scr("boutw1", (c.CI, c.D), BF16), scr("boutw2", (c.CI, c.D), BF16)]
    bxpw = [scr("bxpw1", (c.CI, NXP), BF16), scr("bxpw2", (c.CI, NXP), BF16)]
    bdtpw = [scr("bdtpw1", (c.RDT, c.CI), BF16),
             scr("bdtpw2", (c.RDT, c.CI), BF16)]
    dkind = "ExternalOutput" if debug_outs else "Internal"
    x1_d = nc.dram_tensor("x1", [c.T, c.D], F32, kind=dkind)
    x2_d = nc.dram_tensor("x2", [c.T, c.D], F32, kind=dkind)
    xnTm_d = scr("xnTm", (c.D, c.L), BF16)
    xnTr_d = scr("xnTr", (c.D, c.L), BF16)
    sDel = scr("sDel", (c.CI, c.L), F32)
    sDu = scr("sDu", (c.CI, c.L), F32)
    sSz = scr("sSz", (c.CI, c.L), F32)
    sXd = scr("sXd", (c.CI, c.L), F32)
    sB = scr("sB", (c.NS, c.L), F32)
    sC = scr("sC", (c.NS, c.L), F32)
    yg_d = [scr("yg1", (c.CI, c.L), BF16), scr("yg2", (c.CI, c.L), BF16)]

    ctx = ExitStack()
    e_ = ctx.enter_context
    P = Prog(nc)
    P.init_sems(ctx)

    ident = e_(nc.sbuf_tensor("identsb", [128, 128], BF16))
    antid = e_(nc.sbuf_tensor("antidsb", [128, 128], BF16))
    ggT = e_(nc.sbuf_tensor("ggTsb", [128, KD * c.E], BF16))
    A_sb = e_(nc.sbuf_tensor("A_sb", [128, 2 * KC * c.NS], F32))
    cw_sb = e_(nc.sbuf_tensor("cw_sb", [128, 2 * KC * 4], F32))
    cb_sb = e_(nc.sbuf_tensor("cb_sb", [128, 2 * KC], F32))
    db_sb = e_(nc.sbuf_tensor("db_sb", [128, 2 * KC], F32))
    dv_sb = e_(nc.sbuf_tensor("dv_sb", [128, 2 * KC], F32))
    ones1 = e_(nc.sbuf_tensor("ones1", [1, 128], F32))
    stats = e_(nc.sbuf_tensor("stats", [128, 128], F32))
    glog = e_(nc.sbuf_tensor("glog", [128, TPG * c.E], F32))
    gtmp = e_(nc.sbuf_tensor("gtmp", [128, 2 * TPG * c.E], F32))
    gsc = e_(nc.sbuf_tensor("gsc", [128, 8 * TPG], F32))
    comb = e_(nc.sbuf_tensor("comb", [128, TPG * c.E], F32))

    AR_U = nc.sbuf_bytes_remaining // 4 - 512
    arena = e_(nc.sbuf_tensor("arena", [128, AR_U], F32))

    def af32(off, n):
        assert off + n <= AR_U, (off, n, AR_U)
        return arena[:, off:off + n]

    def abf16(off, nf32):
        assert off + nf32 <= AR_U, (off, nf32, AR_U)
        return arena[:, off:off + nf32].bitcast(BF16)

    pb = [e_(nc.psum_tensor(f"pb{i}", [128, 512], F32)) for i in range(6)]
    pT = e_(nc.psum_tensor("pT", [128, KD * 128], BF16))
    pGt = e_(nc.psum_tensor("pGt", [128, 512], F32))

    prep_ready = {}

    def prep():
        P.dma("sync", "cst", ident[:], ident_d.ap())
        P.dma("sync", "cst", antid[:], antid_d.ap())
        for d in range(2):
            P.dma("sync", "cst", cw_sb[:, d * KC * 4:(d + 1) * KC * 4], convw[d].ap())
            P.dma("sync", "cst", cb_sb[:, d * KC:(d + 1) * KC], convb[d].ap())
            P.dma("sync", "cst", db_sb[:, d * KC:(d + 1) * KC], dtpb[d].ap())
            P.dma("sync", "cst", dv_sb[:, d * KC:(d + 1) * KC], dvec[d].ap())
        v_alog = None
        for d in range(2):
            v_alog = P.dma("sync", "cst",
                           A_sb[:, d * KC * c.NS:(d + 1) * KC * c.NS],
                           alog[d].ap())
        P.run("gpsimd", lambda e: e.memset(ones1[:], 1.0))
        order = []
        for ee in range(c.E):
            order += [(bwg, fwgT, ("wg", ee), ee), (bwu, fwuT, ("wu", ee), ee),
                      (bwd, fwdT, ("wd", ee), ee)]
        order.append((bgT, fgT, ("g", 0), None))
        for d in range(2):
            order += [(binw[d], inwT[d], ("inw", d), None),
                      (boutw[d], outwT[d], ("outw", d), None),
                      (bxpw[d], xpwT[d], ("xpw", d), None),
                      (bdtpw[d], dtpwT[d], ("dtpw", d), None)]
        for dst, src, key, ee in order:
            if key == ("g", 0):
                stream = "cg"
            elif ee is None:
                stream = "cm"
            else:
                stream = f"ce{ee}"
            if ee is None:
                P.dma("gpsimd", stream, dst.ap(), src.ap())
            else:
                P.dma("gpsimd", stream, dst.ap()[ee], src.ap()[ee])
        P.dwait("sync", "cg")
        P.dma("sync", "cst", ggT[:].rearrange("p (k e) -> p k e", e=c.E),
              bgT.ap().rearrange("(k p) e -> p k e", p=128))
        P.dwait("scalar", "cst")
        va = P.run("scalar", lambda e: e.activation(
            A_sb[:], A_sb[:], AF.Exp), drain=True)
        P.wait("vector", "ACT", va)
        P.run("vector", lambda e: e.tensor_scalar_mul(
            A_sb[:], A_sb[:], -1.0), drain=True)

    def ln_tiles(src_d, t0, ntiles, xs_off, consume):
        nD = KD * 128
        s1, s2 = stats[:, 0:16], stats[:, 16:32]
        mm_, vv = stats[:, 32:48], stats[:, 48:64]
        rstd, nmr = stats[:, 64:80], stats[:, 80:96]
        sq = af32(xs_off + 3 * nD, nD)
        rd_v = [None, None]
        sq_v = [None, None]
        va = None
        for j in range(ntiles):
            sl = af32(xs_off + (j % 2) * nD, nD)
            xstream = "xa" if j % 2 == 0 else "xb"
            P.wait("sync", "DVE", rd_v[j % 2])
            P.wait("sync", "ACT", sq_v[j % 2])
            P.dma("sync", xstream, sl,
                  src_d.ap()[t0 + j * 128:t0 + (j + 1) * 128, :])
            P.dwait("vector", xstream)
            rd_v[j % 2] = P.run("vector", lambda e, sl=sl, j=j: e.reduce_sum(
                s1[:, j:j + 1], sl, axis=AX.X), inc=True)
            P.dwait("scalar", xstream)
            P.wait("scalar", "ACT", sq_v[(j - 1) % 2] if j >= 1 else None)
            sq_v[j % 2] = P.run("scalar", lambda e, sl=sl, j=j: e.activation(
                sq, sl, AF.Square, accum_out=s2[:, j:j + 1]), inc=True)
        va = P.drain_inc("scalar")
        vd0 = P.drain_inc("vector")
        P.wait("vector", "ACT", va)
        P.run("vector", lambda e: e.tensor_scalar_mul(
            mm_[:, 0:ntiles], s1[:, 0:ntiles], 1.0 / nD))
        P.run("vector", lambda e: e.tensor_scalar_mul(
            vv[:, 0:ntiles], s2[:, 0:ntiles], 1.0 / nD))
        P.run("vector", lambda e: e.drain())
        P.run("vector", lambda e: e.tensor_tensor(
            s1[:, 0:ntiles], mm_[:, 0:ntiles], mm_[:, 0:ntiles], ALU.mult))
        P.run("vector", lambda e: e.drain())
        P.run("vector", lambda e: e.tensor_sub(
            vv[:, 0:ntiles], vv[:, 0:ntiles], s1[:, 0:ntiles]))
        P.run("vector", lambda e: e.drain())
        vd = P.run("vector", lambda e: e.tensor_scalar_add(
            vv[:, 0:ntiles], vv[:, 0:ntiles], 1e-5), drain=True)
        P.wait("scalar", "DVE", vd)
        va = P.run("scalar", lambda e: e.activation(
            rstd[:, 0:ntiles], vv[:, 0:ntiles], AF.Ln), drain=True)
        P.wait("vector", "ACT", va)
        P.run("vector", lambda e: e.tensor_scalar_mul(
            rstd[:, 0:ntiles], rstd[:, 0:ntiles], -0.5))
        vd = P.drain_inc("vector")
        P.wait("scalar", "DVE", vd)
        va = P.run("scalar", lambda e: e.activation(
            rstd[:, 0:ntiles], rstd[:, 0:ntiles], AF.Exp), drain=True)
        P.wait("vector", "ACT", va)
        P.run("vector", lambda e: e.tensor_mul(
            nmr[:, 0:ntiles], mm_[:, 0:ntiles], rstd[:, 0:ntiles]))
        P.run("vector", lambda e: e.drain())
        vd = P.run("vector", lambda e: e.tensor_scalar_mul(
            nmr[:, 0:ntiles], nmr[:, 0:ntiles], -1.0), drain=True)
        P.wait("scalar", "DVE", vd)
        xn_v = [None, None]
        xread_v = [vd, vd]
        P.wait("sync", "ACT", va)
        P.wait("sync", "DVE", vd)
        xread_v = [None, None]
        for j in range(ntiles):
            sl = af32(xs_off + (j % 2) * nD, nD)
            xstream = "xa" if j % 2 == 0 else "xb"
            xnsl = abf16(xs_off + 2 * nD + (j % 2) * (nD // 2), nD // 2)
            P.wait("sync", "ACT", xread_v[j % 2])
            P.wait("sync", "PE", xn_v[j % 2])
            P.dma("sync", xstream, sl,
                  src_d.ap()[t0 + j * 128:t0 + (j + 1) * 128, :])
            P.dwait("scalar", xstream)
            xread_v[j % 2] = P.run(
                "scalar", lambda e, sl=sl, xnsl=xnsl, j=j: e.activation(
                    xnsl, sl, AF.Identity, bias=nmr[:, j:j + 1],
                    scale=rstd[:, j:j + 1]), drain=True)
            P.wait("tensor", "ACT", xread_v[j % 2])
            xn_v[j % 2] = consume(j, xnsl)

    def emit_gate():
        n = TPG * c.E
        lv = glog[:, 0:n].rearrange("p (j e) -> p j e", e=c.E)
        m1 = gsc[:, 0:TPG]
        m2 = gsc[:, TPG:2 * TPG]
        ed = gsc[:, 2 * TPG:3 * TPG]
        p1h = gsc[:, 3 * TPG:4 * TPG]
        p2h = gsc[:, 4 * TPG:5 * TPG]
        t0v = gtmp[:, 0:n].rearrange("p (j e) -> p j e", e=c.E)
        t1v = gtmp[:, n:2 * n].rearrange("p (j e) -> p j e", e=c.E)
        b_ = lambda a: a[:, :, None].broadcast_to([128, TPG, c.E])
        P.run("vector", lambda e: e.reduce_max(m1[:, 0:TPG], lv, axis=AX.X))
        P.run("vector", lambda e: e.drain())
        P.run("vector", lambda e: e.tensor_tensor(t0v, lv, b_(m1), ALU.is_ge))
        P.run("vector", lambda e: e.drain())
        P.run("vector", lambda e: e.scalar_tensor_tensor(
            t1v, t0v, -1e30, lv, ALU.mult, ALU.add))
        P.run("vector", lambda e: e.drain())
        P.run("vector", lambda e: e.reduce_max(m2[:, 0:TPG], t1v, axis=AX.X))
        P.run("vector", lambda e: e.drain())
        P.run("vector", lambda e: e.tensor_tensor(t1v, t1v, b_(m2), ALU.is_ge))
        P.run("vector", lambda e: e.drain())
        vd = P.run("vector", lambda e: e.tensor_sub(
            m2[:, 0:TPG], m2[:, 0:TPG], m1[:, 0:TPG]), drain=True)
        P.wait("scalar", "DVE", vd)
        va = P.run("scalar", lambda e: e.activation(
            ed[:, 0:TPG], m2[:, 0:TPG], AF.Exp), drain=True)
        P.wait("vector", "ACT", va)
        P.run("vector", lambda e: e.tensor_scalar_add(
            p1h[:, 0:TPG], ed[:, 0:TPG], 1.0))
        P.run("vector", lambda e: e.drain())
        P.run("vector", lambda e: e.reciprocal(p1h[:, 0:TPG], p1h[:, 0:TPG]))
        P.run("vector", lambda e: e.drain())
        P.run("vector", lambda e: e.tensor_mul(
            p2h[:, 0:TPG], ed[:, 0:TPG], p1h[:, 0:TPG]))
        P.run("vector", lambda e: e.drain())
        P.run("vector", lambda e: e.tensor_scalar_mul(
            p1h[:, 0:TPG], p1h[:, 0:TPG], 0.5))
        P.run("vector", lambda e: e.tensor_scalar_mul(
            p2h[:, 0:TPG], p2h[:, 0:TPG], 0.5))
        P.run("vector", lambda e: e.drain())
        cv = comb[:, 0:n].rearrange("p (j e) -> p j e", e=c.E)
        P.run("vector", lambda e: e.tensor_mul(t0v, t0v, b_(p1h)))
        P.run("vector", lambda e: e.tensor_mul(t1v, t1v, b_(p2h)))
        P.run("vector", lambda e: e.drain())
        return P.run("vector", lambda e: e.tensor_add(cv, t0v, t1v), drain=True)

    u_xnt = KD * c.FF_G // 2
    u_acc = TPG * c.D
    u_hid = KH * c.FF_G // 2
    u_qp = 2 * (KD * HQ) // 2
    u_w = max(2 * u_qp, KH * c.D // 2)
    u_xs = 4 * KD * 128
    FF_XNT = 0
    FF_ACC = FF_XNT + u_xnt
    FF_HID = FF_ACC + u_acc
    FF_W = FF_HID + u_hid
    FF_XS = FF_W + u_w
    FF_SIL = FF_XS + u_xs
    u_sil = 2 * FTS
    assert FF_SIL + u_sil <= AR_U

    state = {"g_ev": [None, None], "u_ev": [None, None], "o_ev": [None, None],
             "m3_last": None, "hid_last": None, "gate_act": None}

    def ffn_call(call):
        src_d = x_d if call == 0 else x2_d
        dst_d = x1_d if call == 0 else out_d
        xnT = abf16(FF_XNT, u_xnt)
        hid = abf16(FF_HID, u_hid)
        acc = af32(FF_ACC, u_acc)
        wq = [abf16(FF_W + s * u_qp, u_qp) for s in range(2)]
        wd_all = abf16(FF_W, KH * c.D // 2)
        xnTv = xnT.rearrange("p (k t) -> p k t", t=c.FF_G)
        hidv = hid.rearrange("p (kh t) -> p kh t", t=c.FF_G)
        wdv = wd_all.rearrange("p (k d) -> p k d", k=KH)
        accv = acc.rearrange("p (m d) -> p m d", m=TPG)
        for g in range(NG):
            t0 = g * c.FF_G
            last_xnt = [None]

            def consume(j, xnsl):
                def tfn(e, xnsl=xnsl):
                    for k in range(KD):
                        ins = e.transpose(
                            pT[:, k * 128:(k + 1) * 128],
                            xnsl.rearrange("p (k t) -> p k t", k=KD)[:, k],
                            ident[:])
                    return ins
                vp = P.run("tensor", tfn, inc=True)
                P.wait("vector", "PE", vp)
                vd = P.run("vector", lambda e, j=j: e.tensor_copy(
                    xnTv[:, :, j * 128:(j + 1) * 128],
                    pT[:, 0:KD * 128].rearrange("p (k t) -> p k t", k=KD)),
                    drain=True)
                last_xnt[0] = vd
                P.wait("tensor", "DVE", vd)

                def gmm(e, j=j):
                    for k in range(KD):
                        ins = e.matmul(
                            pGt[:, 0:c.E],
                            xnTv[:, k, j * 128:(j + 1) * 128],
                            ggT[:].rearrange("p (k e) -> p k e", e=c.E)[:, k],
                            start=(k == 0), stop=(k == KD - 1))
                    return ins
                vp2 = P.run("tensor", gmm, inc=True)
                P.wait("scalar", "PE", vp2)
                va = P.run("scalar", lambda e, j=j: e.copy(
                    glog[:, j * c.E:(j + 1) * c.E], pGt[:, 0:c.E]), drain=True)
                state["gate_act"] = va
                P.wait("tensor", "ACT", va)
                return vp2

            ln_tiles(src_d, t0, TPG, FF_XS, consume)
            P.wait("vector", "ACT", state["gate_act"])
            emit_gate()
            for ee in range(c.E):
                P.wait("sync", "PE", state["m3_last"])
                P.dwait("sync", f"ce{ee}")
                qv = [None] * 4
                pe_q = [None] * 4

                def load_quarter(q):
                    sl = wq[q % 2]
                    ws = "wqa" if q % 2 == 0 else "wqb"
                    slv = sl.rearrange("p (m k h) -> p m k h", m=2, k=KD)
                    for k in range(KD):
                        P.dma("sync", ws, slv[:, 0, k],
                              bwg.ap()[ee, k * 128:(k + 1) * 128,
                                       q * HQ:(q + 1) * HQ])
                    for k in range(KD):
                        P.dma("sync", ws, slv[:, 1, k],
                              bwu.ap()[ee, k * 128:(k + 1) * 128,
                                       q * HQ:(q + 1) * HQ])

                load_quarter(0)
                load_quarter(1)
                P.wait("vector", "PE", state["m3_last"])
                ht_glob = 0
                for q in range(4):
                    P.dwait("tensor", "wqa" if q % 2 == 0 else "wqb")
                    if q == 0:
                        P.wait("tensor", "DVE", last_xnt[0])
                    slv = wq[q % 2].rearrange("p (m k h) -> p m k h",
                                              m=2, k=KD)
                    for ht in range(NHT):
                        kidx = q * NHT + ht
                        for ts_ in range(NTS):
                            sg = ht_glob % 2
                            pg, pu = pb[sg], pb[2 + sg]
                            P.wait("tensor", "ACT", state["g_ev"][sg])
                            P.wait("tensor", "DVE", state["u_ev"][sg])

                            def m1(e, slv=slv, ht=ht, ts_=ts_, pg=pg):
                                for k in range(KD):
                                    ins = e.matmul(
                                        pg[:, 0:FTS],
                                        slv[:, 0, k, ht * 128:(ht + 1) * 128],
                                        xnTv[:, k, ts_ * FTS:(ts_ + 1) * FTS],
                                        start=(k == 0), stop=(k == KD - 1))
                                return ins
                            v1 = P.run("tensor", m1, inc=True)

                            def m2(e, slv=slv, ht=ht, ts_=ts_, pu=pu):
                                for k in range(KD):
                                    ins = e.matmul(
                                        pu[:, 0:FTS],
                                        slv[:, 1, k, ht * 128:(ht + 1) * 128],
                                        xnTv[:, k, ts_ * FTS:(ts_ + 1) * FTS],
                                        start=(k == 0), stop=(k == KD - 1))
                                return ins
                            v2 = P.run("tensor", m2, inc=True)
                            pe_q[q] = v2
                            P.wait("scalar", "PE", v1)
                            sil = af32(FF_SIL + sg * FTS, FTS)
                            va = P.run("scalar",
                                       lambda e, pg=pg, sil=sil: e.activation(
                                           sil, pg[:, 0:FTS], AF.Silu),
                                       drain=True)
                            state["g_ev"][sg] = va
                            P.wait("vector", "ACT", va)
                            P.wait("vector", "PE", v2)
                            hcols = hidv[:, kidx, ts_ * FTS:(ts_ + 1) * FTS]
                            vd = P.run("vector",
                                       lambda e, sil=sil, pu=pu, hcols=hcols:
                                       e.tensor_tensor(hcols, sil,
                                                       pu[:, 0:FTS],
                                                       ALU.mult), inc=True)
                            state["u_ev"][sg] = vd
                            ht_glob += 1
                    if q + 2 <= 3:
                        P.wait("sync", "PE", pe_q[q])
                        load_quarter(q + 2)
                state["hid_last"] = P.drain_inc("vector")
                P.wait("sync", "PE", pe_q[3])
                for k in range(KH):
                    P.dma("sync", "wdl", wdv[:, k],
                          bwd.ap()[ee, k * 128:(k + 1) * 128, :])
                P.dwait("tensor", "wdl")
                P.wait("tensor", "DVE", state["hid_last"])
                for m in range(TPG):
                    for f in range(NF):
                        so = (m * NF + f) % 2
                        po = pb[4 + so]
                        P.wait("tensor", "DVE", state["o_ev"][so])

                        def m3(e, m=m, f=f, po=po):
                            for k in range(KH):
                                ins = e.matmul(
                                    po[:, 0:FD],
                                    hidv[:, k, m * 128:(m + 1) * 128],
                                    wdv[:, k, f * FD:(f + 1) * FD],
                                    start=(k == 0), stop=(k == KH - 1))
                            return ins
                        vp = P.run("tensor", m3, inc=True)
                        state["m3_last"] = vp
                        P.wait("vector", "PE", vp)
                        accsl = accv[:, m, f * FD:(f + 1) * FD]
                        ccol = comb[:, m * c.E + ee:m * c.E + ee + 1]
                        if ee == 0:
                            vd = P.run("vector",
                                       lambda e, accsl=accsl, po=po,
                                       ccol=ccol: e.tensor_scalar(
                                           accsl, po[:, 0:FD], ccol, None,
                                           ALU.mult), inc=True)
                        else:
                            vd = P.run("vector",
                                       lambda e, accsl=accsl, po=po,
                                       ccol=ccol: e.scalar_tensor_tensor(
                                           accsl, po[:, 0:FD], ccol, accsl,
                                           ALU.mult, ALU.add), inc=True)
                        state["o_ev"][so] = vd
            vacc = P.drain_inc("vector")
            fin_rd = [vacc, vacc]
            for j in range(TPG):
                sl = af32(FF_XS + (j % 2) * KD * 128, KD * 128)
                xstream = "xa" if j % 2 == 0 else "xb"
                P.wait("sync", "DVE", fin_rd[j % 2])
                P.dwait("sync", "sfo")
                P.dma("sync", xstream, sl,
                      src_d.ap()[t0 + j * 128:t0 + (j + 1) * 128, :])
                P.dwait("vector", xstream)
                vd = P.run("vector", lambda e, sl=sl, j=j: e.tensor_add(
                    sl, sl, accv[:, j]), drain=True)
                fin_rd[j % 2] = vd
                P.wait("gpsimd", "DVE", vd)
                P.dma(
                    "gpsimd", "sfo",
                    dst_d.ap()[t0 + j * 128:t0 + (j + 1) * 128, :], sl)
            P.barrier()

    u_xi = KC * (c.TA + 3)
    u_xc = KC * c.TA
    u_xcb = KC * c.TA // 2
    u_xnt_m = KD * c.TA // 2
    u_inw = KD * c.C2 // 2
    u_xpw = (KC * NXP + 1) // 2
    u_dtpw = c.CI // 2
    u_dtbf = c.TA // 2
    u_stg = 10 * c.TA
    MA_XI = 0
    MA_XC = MA_XI + u_xi
    MA_XCB = MA_XC + u_xc
    MA_XNT = MA_XCB + u_xcb
    MA_INW = MA_XNT + u_xnt_m
    MA_XPW = MA_INW + u_inw
    MA_DTPW = MA_XPW + u_xpw
    MA_DTBF = MA_DTPW + u_dtpw
    MA_STG = MA_DTBF + u_dtbf
    MA_BST = MA_STG + u_stg
    assert MA_BST + 2 * c.TA <= AR_U

    u_mb = KC * c.TB
    u_br = c.NS * c.TB
    MB_DEL = 0
    MB_DU = MB_DEL + u_mb
    MB_SZ = MB_DU + u_mb
    MB_XD = MB_SZ + u_mb
    MB_BR = MB_XD + u_mb
    MB_CR = MB_BR + u_br
    MB_S = MB_CR + u_br
    MB_DA = MB_S + 2 * u_br
    MB_DB = MB_DA + u_br
    MB_YTF = MB_DB + 2 * u_br
    MB_YGF = MB_DA
    MB_H = MB_YTF + u_mb
    assert MB_H + KC * c.NS <= AR_U

    u_ygm = 2 * (KC * 128 // 2)
    u_ow = 2 * (KC * c.D // 2)
    MC_YG = 0
    MC_OW = MC_YG + u_ygm
    MC_X1 = MC_OW + u_ow
    MC_X2 = MC_X1 + 2 * KD * 128
    assert MC_X2 + 2 * KD * 128 <= AR_U

    def mamba_batch(b):
        stg = abf16(MA_STG, KD * 128)
        stgv = stg.rearrange("p (r k t) -> p r k t", r=2, k=KD)
        st_v = [[None, None], [None, None]]

        def consume(j, xnsl):
            vp_last = None
            for rev in range(2):
                idm = antid if rev else ident

                def tfn(e, xnsl=xnsl, idm=idm):
                    for k in range(KD):
                        ins = e.transpose(
                            pT[:, k * 128:(k + 1) * 128],
                            xnsl.rearrange("p (k t) -> p k t", k=KD)[:, k],
                            idm[:])
                    return ins
                vp = P.run("tensor", tfn, inc=True)
                vp_last = vp
                P.wait("vector", "PE", vp)
                P.dwait("vector", "slnm" if rev == 0 else "slnr")
                dst = stgv[:, rev]
                vd = P.run("vector", lambda e, dst=dst: e.tensor_copy(
                    dst, pT[:, 0:KD * 128].rearrange("p (k t) -> p k t",
                                                     k=KD)), drain=True)
                P.wait("gpsimd", "DVE", vd)
                jj = j if rev == 0 else (LT - 1 - j)
                tgt = xnTm_d if rev == 0 else xnTr_d
                P.dma(
                    "gpsimd", "slnm" if rev == 0 else "slnr",
                    tgt.ap().rearrange("(k p) t -> p k t", p=128)
                       [:, :, jj * 128:(jj + 1) * 128],
                    dst)
                P.wait("tensor", "DVE", vd)
            return vp_last

        ln_tiles(x1_d, b * c.L, LT, MA_STG + KD * 128, consume)
        P.barrier()
        for dirn in range(2):
            mamba_front(b, dirn)
            P.barrier()
            mamba_scan(b, dirn)
            P.barrier()
        mamba_out(b)
        P.barrier()

    def mamba_front(b, dirn):
        xnsrc = xnTm_d if dirn == 0 else xnTr_d
        inw = abf16(MA_INW, u_inw)
        inwv = inw.rearrange("p (k m) -> p k m", k=KD)
        dtpw = abf16(MA_DTPW, u_dtpw)
        xi = af32(MA_XI, u_xi)
        xiv = xi.rearrange("p (ct t) -> p ct t", ct=KC)
        xc = af32(MA_XC, u_xc)
        xcv = xc.rearrange("p (ct t) -> p ct t", ct=KC)
        xcb = abf16(MA_XCB, u_xcb)
        xcbv = xcb.rearrange("p (k t) -> p k t", k=KC)
        dtbf = abf16(MA_DTBF, u_dtbf)
        P.dwait("sync", "cm")
        for k in range(KD):
            P.dma("sync", "mw", inwv[:, k],
                  binw[dirn].ap()[k * 128:(k + 1) * 128, :])
        xpw_full = abf16(MA_XPW, u_xpw)
        for k in range(KC):
            P.dma("sync", "mw",
                  xpw_full[:, k * NXP:(k + 1) * NXP],
                  bxpw[dirn].ap()[k * 128:(k + 1) * 128, :])
        P.dma("sync", "mw", dtpw[0:c.RDT, 0:c.CI], bdtpw[dirn].ap())
        vz = P.run("gpsimd", lambda e: e.memset(xiv[:, :, 0:3], 0.0),
                   drain=True)
        P.wait("scalar", "PL", vz)
        pe_ev = [None, None]
        dtp_ev = [None, None]
        sz_st = [None, None]
        xd_st = [None, None]
        dl_st = [None, None]
        du_st = [None, None]
        du_rd = [None, None]
        bc_st = None
        prev_vcv = None
        xz_last = None
        xp_last = None
        for ch in range(NCA):
            tt0 = ch * c.TA
            xnt = abf16(MA_XNT, u_xnt_m)
            xntv = xnt.rearrange("p (k t) -> p k t", k=KD)
            P.wait("sync", "PE", xz_last)
            for k in range(KD):
                P.dma("sync", "xnt", xntv[:, k],
                      xnsrc.ap()[k * 128:(k + 1) * 128, tt0:tt0 + c.TA])
            P.dwait("tensor", "xnt")
            if ch == 0:
                P.dwait("tensor", "mw")
            if ch > 0:
                P.wait("scalar", "DVE", prev_vcv)
                P.run("scalar", lambda e: e.copy(
                    xiv[:, :, 0:3], xiv[:, :, c.TA:c.TA + 3]))
                P.run("scalar", lambda e: e.drain())
            for m in range(2 * KC):
                sg = m % 2
                pg = pb[sg]
                g1 = max(pe_ev[sg] or 0, (dtp_ev[0] or 0) if sg == 0 else 0)
                P.wait("tensor", "ACT", g1 if g1 else None)

                def mmx(e, m=m, pg=pg):
                    for k in range(KD):
                        ins = e.matmul(
                            pg[:, 0:c.TA],
                            inwv[:, k, m * 128:(m + 1) * 128],
                            xntv[:, k],
                            start=(k == 0), stop=(k == KD - 1))
                    return ins
                vp = P.run("tensor", mmx, inc=True)
                xz_last = vp
                P.wait("scalar", "PE", vp)
                if m < KC:
                    dst = xiv[:, m, 3:3 + c.TA]
                    va = P.run("scalar", lambda e, dst=dst, pg=pg:
                               e.copy(dst, pg[:, 0:c.TA]), inc=True)
                else:
                    P.dwait("scalar", "ssz")
                    szst = af32(MA_STG + 6 * c.TA + sg * c.TA, c.TA)
                    va = P.run("scalar", lambda e, szst=szst, pg=pg:
                               e.activation(szst, pg[:, 0:c.TA], AF.Silu),
                               drain=True)
                    P.wait("gpsimd", "ACT", va)
                    P.dma(
                        "gpsimd", "ssz",
                        sSz.ap()[(m - KC) * 128:(m - KC + 1) * 128,
                                 tt0:tt0 + c.TA], szst)
                pe_ev[sg] = va
            vxi = P.drain_inc("scalar")
            P.wait("vector", "ACT", vxi)
            for ct in range(KC):
                xitile = xiv[:, ct]
                xct = xcv[:, ct]
                cwc = cw_sb[:, dirn * KC * 4 + ct * 4:dirn * KC * 4 + ct * 4 + 4]

                def conv(e, xitile=xitile, xct=xct, cwc=cwc):
                    e.tensor_scalar(xct, xitile[:, 0:c.TA], cwc[:, 0:1],
                                    None, ALU.mult)
                    ins = None
                    for jj in range(1, 4):
                        e.drain()
                        ins = e.scalar_tensor_tensor(
                            xct, xitile[:, jj:jj + c.TA], cwc[:, jj:jj + 1],
                            xct, ALU.mult, ALU.add)
                    return ins
                P.run("vector", conv)
            vcv = P.drain_inc("vector")
            prev_vcv = vcv
            P.wait("scalar", "DVE", vcv)
            for ct in range(KC):
                xct = xcv[:, ct]
                P.run("scalar", lambda e, xct=xct, ct=ct: e.activation(
                    xct, xct, AF.Silu,
                    bias=cb_sb[:, dirn * KC + ct:dirn * KC + ct + 1]))
                P.run("scalar", lambda e: e.drain())
                P.dwait("scalar", "sxd")
                xdst = af32(MA_STG + 8 * c.TA + (ct % 2) * c.TA, c.TA)
                va = P.run("scalar", lambda e, xct=xct, ct=ct, xdst=xdst:
                           e.activation(xdst, xct, AF.Identity,
                                        scale=dv_sb[:, dirn * KC + ct:
                                                    dirn * KC + ct + 1]),
                           drain=True)
                P.wait("gpsimd", "ACT", va)
                P.dma(
                    "gpsimd", "sxd", sXd.ap()[ct * 128:(ct + 1) * 128,
                                              tt0:tt0 + c.TA], xdst)
            vsl = P.drain_inc("scalar")
            P.wait("vector", "ACT", vsl)
            P.wait("vector", "PE", xp_last)
            vcb = P.run("vector", lambda e: e.tensor_copy(xcb, xc),
                        drain=True)
            P.wait("tensor", "DVE", vcb)
            parts = [(0, c.RDT, pb[2]), (c.RDT, c.NS, pb[3]),
                     (c.RDT + c.NS, c.NS, pb[5])]
            vp = None
            for (off, mlen, pp) in parts:
                def mmxp(e, off=off, mlen=mlen, pp=pp):
                    ins = None
                    for k in range(KC):
                        ins = e.matmul(
                            pp[0:mlen, 0:c.TA],
                            xpw_full[:, k * NXP + off:k * NXP + off + mlen],
                            xcbv[:, k],
                            start=(k == 0), stop=(k == KC - 1))
                    return ins
                vp = P.run("tensor", mmxp, inc=True)
            xp_last = vp
            P.wait("scalar", "PE", vp)
            P.dwait("scalar", "sbc")
            va = P.run("scalar", lambda e: e.copy(
                dtbf[0:c.RDT, 0:c.TA], pb[2][0:c.RDT, 0:c.TA]), drain=True)
            bst = af32(MA_BST, c.TA)
            cst = af32(MA_BST + c.TA, c.TA)
            P.run("scalar", lambda e, bst=bst: e.copy(
                bst[0:c.NS, :], pb[3][0:c.NS, 0:c.TA]))
            P.run("scalar", lambda e: e.drain())
            vc = P.run("scalar", lambda e, cst=cst: e.copy(
                cst[0:c.NS, :], pb[5][0:c.NS, 0:c.TA]), drain=True)
            P.wait("gpsimd", "ACT", vc)
            P.dma("gpsimd", "sbc", sB.ap()[:, tt0:tt0 + c.TA],
                  bst[0:c.NS, :])
            P.dma("gpsimd", "sbc", sC.ap()[:, tt0:tt0 + c.TA],
                  cst[0:c.NS, :])
            P.wait("tensor", "ACT", va)
            for m in range(KC):
                sg = m % 2
                pp = pb[sg * 4]
                P.wait("tensor", "ACT", dtp_ev[sg])

                def mmd(e, m=m, pp=pp):
                    return e.matmul(
                        pp[:, 0:c.TA],
                        dtpw[0:c.RDT, m * 128:(m + 1) * 128],
                        dtbf[0:c.RDT, 0:c.TA],
                        start=True, stop=True)
                vp = P.run("tensor", mmd, inc=True)
                P.wait("scalar", "PE", vp)
                est = af32(MA_STG + sg * c.TA, c.TA)
                dst_ = af32(MA_STG + 2 * c.TA + sg * c.TA, c.TA)
                dust = af32(MA_STG + 4 * c.TA + sg * c.TA, c.TA)
                P.dwait("scalar", "sdl")
                P.wait("scalar", "DVE", du_rd[sg])
                P.run("scalar", lambda e, est=est, pp=pp, m=m: e.activation(
                    est, pp[:, 0:c.TA], AF.Exp,
                    bias=db_sb[:, dirn * KC + m:dirn * KC + m + 1]))
                P.run("scalar", lambda e: e.drain())
                va = P.run("scalar", lambda e, est=est, dst_=dst_:
                           e.activation(dst_, est, AF.Ln, bias=1.0),
                           drain=True)
                dtp_ev[sg] = va
                P.wait("gpsimd", "ACT", va)
                P.dma(
                    "gpsimd", "sdl", sDel.ap()[m * 128:(m + 1) * 128,
                                               tt0:tt0 + c.TA], dst_)
                P.wait("vector", "ACT", va)
                P.dwait("vector", "sdu")
                xct = xcv[:, m]
                vd = P.run("vector", lambda e, dst_=dst_, dust=dust, xct=xct:
                           e.tensor_mul(dust, dst_, xct), drain=True)
                du_rd[sg] = vd
                P.wait("gpsimd", "DVE", vd)
                P.dma(
                    "gpsimd", "sdu", sDu.ap()[m * 128:(m + 1) * 128,
                                              tt0:tt0 + c.TA], dust)

    def mamba_scan(b, dirn):
        for s in ("ssz", "sxd", "sdl", "sdu", "sbc"):
            P.dwait("sync", s)
        delS = af32(MB_DEL, u_mb)
        delv = delS.rearrange("p (ct t) -> p ct t", ct=KC)
        duS = af32(MB_DU, u_mb)
        duv = duS.rearrange("p (ct t) -> p ct t", ct=KC)
        szS = af32(MB_SZ, u_mb)
        xdS = af32(MB_XD, u_mb)
        brS = af32(MB_BR, u_br)
        crS = af32(MB_CR, u_br)
        daF = af32(MB_DA, u_br)
        ytF = af32(MB_YTF, u_mb)
        ygF = abf16(MB_YGF, u_mb // 2)
        hst = af32(MB_H, KC * c.NS)
        v_scan = {}
        v_red = {}
        g_dbu = {}
        g_mul = {}
        a_da = {}
        a_hc = {}
        prev_vd = None
        NCT = NCB * KC

        def S_of(g):
            return af32(MB_S + (g % 2) * u_br, u_br)

        def db_of(g):
            return af32(MB_DB + (g % 2) * u_br, u_br)

        for cb_ in range(NCB):
            tt0 = cb_ * c.TB
            if cb_ > 0:
                g0 = (cb_ - 1) * KC
                P.wait("sync", "ACT", a_da.get(g0 + KC - 1))
                P.wait("sync", "PL", g_mul.get(g0 + KC - 1))
                P.wait("sync", "DVE", prev_vd)
            P.dma("sync", "mbl", delv,
                  sDel.ap().rearrange("(ct p) t -> p ct t", p=128)
                      [:, :, tt0:tt0 + c.TB])
            P.dma("sync", "mbl", duv,
                  sDu.ap().rearrange("(ct p) t -> p ct t", p=128)
                      [:, :, tt0:tt0 + c.TB])
            P.dma("sync", "mbl", szS.rearrange("p (ct t) -> p ct t", ct=KC),
                  sSz.ap().rearrange("(ct p) t -> p ct t", p=128)
                      [:, :, tt0:tt0 + c.TB])
            P.dma("sync", "mbl", xdS.rearrange("p (ct t) -> p ct t", ct=KC),
                  sXd.ap().rearrange("(ct p) t -> p ct t", p=128)
                      [:, :, tt0:tt0 + c.TB])
            P.dma("sync", "mbl",
                  brS.rearrange("p (n t) -> p n t", n=c.NS),
                  sB.ap()[:, tt0:tt0 + c.TB].unsqueeze(0)
                     .broadcast_to([128, c.NS, c.TB]))
            P.dma("sync", "mbl",
                  crS.rearrange("p (n t) -> p n t", n=c.NS),
                  sC.ap()[:, tt0:tt0 + c.TB].unsqueeze(0)
                     .broadcast_to([128, c.NS, c.TB]))
            P.dwait("gpsimd", "mbl")
            P.dwait("scalar", "mbl")
            P.dwait("vector", "mbl")
            if cb_ == 0:
                P.dwait("scalar", "syg")
            def emit_dbu(g):
                ct = g % KC
                duT = duv[:, ct]
                dbF = db_of(g)
                P.wait("gpsimd", "DVE", v_red.get(g - 2))
                g_dbu[g] = P.run("gpsimd", lambda e, duT=duT, dbF=dbF:
                                 e.tensor_tensor(
                                     dbF.rearrange("p (n t) -> p n t",
                                                   n=c.NS),
                                     duT[:, None, :]
                                     .broadcast_to([128, c.NS, c.TB]),
                                     brS.rearrange("p (n t) -> p n t",
                                                   n=c.NS),
                                     ALU.mult), inc=True)

            def emit_reduce(g):
                dbF = db_of(g)
                ct = g % KC
                P.wait("vector", "PL", g_mul[g])
                v_red[g] = P.run("vector", lambda e, dbF=dbF, ct=ct:
                                 e.reduce_sum(
                                     ytF.rearrange("p (ct t) -> p ct t",
                                                   ct=KC)[:, ct],
                                     dbF.rearrange("p (n t) -> p t n",
                                                   t=c.TB),
                                     axis=AX.X), inc=True)

            emit_dbu(cb_ * KC)
            for ct in range(KC):
                g = cb_ * KC + ct
                delT = delv[:, ct]
                S_ = S_of(g)
                Sv = S_.rearrange("p (n t) -> p n t", n=c.NS)
                dbF = db_of(g)
                P.wait("scalar", "DVE", v_scan.get(g - 1))
                va = None
                for n in range(c.NS):
                    acol = A_sb[:, dirn * KC * c.NS + ct * c.NS + n:
                                dirn * KC * c.NS + ct * c.NS + n + 1]
                    va = P.run("scalar", lambda e, delT=delT, n=n, acol=acol:
                               e.activation(
                                   daF[:, n * c.TB:(n + 1) * c.TB], delT,
                                   AF.Exp, scale=acol),
                               inc=(n == c.NS - 1))
                a_da[g] = va
                P.wait("vector", "ACT", a_da[g])
                P.wait("vector", "PL", g_dbu[g])
                P.wait("vector", "ACT", a_hc.get(g - 2))
                P.wait("vector", "PL", g_mul.get(g - 2))
                for n in range(c.NS):
                    init = 0.0 if cb_ == 0 else hst[:, ct * c.NS + n:
                                                    ct * c.NS + n + 1]
                    v_scan[g] = P.run(
                        "vector", lambda e,
                        so=S_[:, n * c.TB:(n + 1) * c.TB],
                        da_=daF[:, n * c.TB:(n + 1) * c.TB],
                        db_=dbF[:, n * c.TB:(n + 1) * c.TB],
                        init=init: e.tensor_tensor_scan(
                            so, da_, db_, init, ALU.mult, ALU.add),
                        inc=(n == c.NS - 1)) or v_scan.get(g)
                if ct > 0:
                    emit_reduce(g - 1)
                if ct + 1 < KC:
                    emit_dbu(g + 1)
                P.wait("scalar", "DVE", v_scan[g])
                a_hc[g] = P.run("scalar", lambda e, Sv=Sv, ct=ct: e.copy(
                    hst[:, ct * c.NS:(ct + 1) * c.NS],
                    Sv[:, :, c.TB - 1]), inc=True)
                P.wait("gpsimd", "DVE", v_scan[g])
                g_mul[g] = P.run("gpsimd", lambda e, S_=S_, dbF=dbF:
                                 e.tensor_mul(dbF, S_, crS), inc=True)
            emit_reduce(cb_ * KC + KC - 1)
            P.run("vector", lambda e: e.drain())
            P.dwait("vector", "syg")
            P.wait("vector", "ACT", a_hc[cb_ * KC + KC - 1])
            P.run("vector", lambda e: e.tensor_add(ytF, ytF, xdS))
            P.run("vector", lambda e: e.drain())
            vd = P.run("vector", lambda e: e.tensor_mul(ygF, ytF, szS),
                       drain=True)
            prev_vd = vd
            P.wait("gpsimd", "DVE", vd)
            P.dma("gpsimd", "syg",
                  yg_d[dirn].ap().rearrange("(ct p) t -> p ct t", p=128)
                      [:, :, tt0:tt0 + c.TB],
                  ygF.rearrange("p (ct t) -> p ct t", ct=KC))
            if cb_ + 1 < NCB:
                P.dwait("scalar", "syg")

    def mamba_out(b):
        P.dwait("sync", "syg")
        ow = abf16(MC_OW, u_ow)
        owv = ow.rearrange("p (d k md) -> p d k md", d=2, k=KC)
        ygm = abf16(MC_YG, u_ygm)
        ygmv = ygm.rearrange("p (d ct t) -> p d ct t", d=2, ct=KC)
        for dirn in range(2):
            for k in range(KC):
                P.dma("sync", "owl", owv[:, dirn, k],
                      boutw[dirn].ap()[k * 128:(k + 1) * 128, :])
        o_ev = [None, None]
        x2_st = [None, None]
        x1_rd = [None, None]
        ym_last = None
        for m in range(LT):
            P.wait("sync", "PE", ym_last)
            for dirn in range(2):
                P.dma("sync", "ygl", ygmv[:, dirn],
                      yg_d[dirn].ap().rearrange("(ct p) t -> p ct t", p=128)
                      [:, :, m * 128:(m + 1) * 128])
            x1sl = af32(MC_X1 + (m % 2) * KD * 128, KD * 128)
            x1s = "x1a" if m % 2 == 0 else "x1b"
            P.wait("sync", "DVE", x1_rd[m % 2])
            P.dma("sync", x1s, x1sl,
                  x1_d.ap()[b * c.L + m * 128:b * c.L + (m + 1) * 128, :])
            P.dwait("tensor", "ygl")
            if m == 0:
                P.dwait("tensor", "owl")
            P.dwait("vector", "sx2")
            x2sl = af32(MC_X2 + (m % 2) * KD * 128, KD * 128)
            for f in range(NF):
                so = f % 2
                po = pb[so]
                P.wait("tensor", "DVE", o_ev[so])

                def mmo(e, f=f, po=po):
                    first = True
                    ins = None
                    for dirn in range(2):
                        for k in range(KC):
                            ins = e.matmul(
                                po[:, 0:FD],
                                ygmv[:, dirn, k],
                                owv[:, dirn, k, f * FD:(f + 1) * FD],
                                start=first,
                                stop=(dirn == 1 and k == KC - 1))
                            first = False
                    return ins
                vp = P.run("tensor", mmo, inc=True)
                ym_last = vp
                P.wait("vector", "PE", vp)
                P.dwait("vector", x1s)
                vd = P.run("vector", lambda e, f=f, po=po, x1sl=x1sl,
                           x2sl=x2sl: e.tensor_add(
                               x2sl[:, f * FD:(f + 1) * FD],
                               x1sl[:, f * FD:(f + 1) * FD], po[:, 0:FD]),
                           inc=True)
                o_ev[so] = vd
            vfin = P.drain_inc("vector")
            x1_rd[m % 2] = vfin
            P.wait("gpsimd", "DVE", vfin)
            P.dma(
                "gpsimd", "sx2",
                x2_d.ap()[b * c.L + m * 128:b * c.L + (m + 1) * 128, :],
                x2sl)

    prep()
    P.barrier()
    ffn_call(0)
    P.barrier()
    for b in range(c.BL):
        mamba_batch(b)
    P.barrier()
    ffn_call(1)
    for s in ("sfo", "sx2", "syg"):
        P.dwait("gpsimd", s)

    P.emit_all()
    ctx.close()
    return nc


def prep_inputs(cfg, inputs, ncores=NCORES):
    c = cfg
    KC = c.CI // 128
    g = lambda k: np.asarray(inputs[k])
    x = g("L0_attn_x").reshape(ncores, c.BL * c.L, c.D)
    T = lambda a: np.ascontiguousarray(np.swapaxes(a, -1, -2))
    common = {
        "fwgT": T(g("L0_ffn_wg")),
        "fwuT": T(g("L0_ffn_wu")),
        "fwdT": T(g("L0_ffn_wd")),
        "fgT": T(g("L0_ffn_gate_w")),
        "ident": np.eye(128, dtype=ml_dtypes.bfloat16),
        "antid": np.ascontiguousarray(
            np.flipud(np.eye(128)).astype(ml_dtypes.bfloat16)),
    }
    for sfx in ("1", "2"):
        common[f"inwT{sfx}"] = T(g(f"L0_attn_in_w{sfx}"))
        common[f"outwT{sfx}"] = T(g(f"L0_attn_out_w{sfx}"))
        common[f"xpwT{sfx}"] = T(g(f"L0_attn_xp_w{sfx}"))
        common[f"dtpwT{sfx}"] = T(g(f"L0_attn_dtp_w{sfx}"))
        common[f"convw{sfx}"] = np.ascontiguousarray(
            g(f"L0_attn_conv_w{sfx}").reshape(c.CI, 4)
            .reshape(KC, 128, 4).transpose(1, 0, 2).reshape(128, KC * 4))
        common[f"convb{sfx}"] = np.ascontiguousarray(
            g(f"L0_attn_conv_b{sfx}").reshape(KC, 128).T)
        common[f"dtpb{sfx}"] = np.ascontiguousarray(
            g(f"L0_attn_dtp_b{sfx}").reshape(KC, 128).T)
        common[f"dvec{sfx}"] = np.ascontiguousarray(
            g(f"L0_attn_Dvec{sfx}").reshape(KC, 128).T)
        common[f"alog{sfx}"] = np.ascontiguousarray(
            g(f"L0_attn_A_log{sfx}").reshape(KC, 128, c.NS)
            .transpose(1, 0, 2).reshape(128, KC * c.NS))
    in_maps = []
    for i in range(ncores):
        m = dict(common)
        m["x"] = np.ascontiguousarray(x[i])
        in_maps.append(m)
    return in_maps


def _install_fast_compile():
    try:
        import concourse.bass_utils as bu
        if getattr(bu, "_fast_compile_installed", False):
            return
        from pathlib import Path as _Path

        def bir_verify_and_optimise(tmpdir, inp="bir.json", outp="file.neff",
                                    arch=None, *, dve_root=None):
            cmd = [
                bu.get_walrus_driver(),
                "--pass",
                ",".join([
                    "runtime_memory_reservation",
                    "lower_act",
                    "lower_dve",
                    "lower_ap_offset",
                    "codegen",
                    "neff_packager",
                ]),
                "-i", inp,
                "--neff-output-filename", outp,
                "--enable-birsim=true",
                "--mem-mode=physical",
                "--policy=0",
                "--enable-ldw-opt=false",
                "--assign-static-dmas-to-sp=false",
                "--dram-page-size=256",
                "--enable-neff-debug-info=true",
                "--jobs", "8",
                *bu.get_walrus_args(
                    bu.get_bir_arch(tmpdir, inp) if arch is None else arch,
                    tmpdir, dve_root=dve_root),
            ]
            result = bu.run_command(cmd, cwd=tmpdir)
            if result is not None:
                (_Path(tmpdir) / "log.txt").write_text(result.stdout)
            return str(_Path(tmpdir) / outp)

        bu.bir_verify_and_optimise = bir_verify_and_optimise
        bu._fast_compile_installed = True
    except Exception:
        pass


def kernel(**inputs):
    _install_fast_compile()
    cfg = FULL
    in_maps = prep_inputs(cfg, inputs)
    nc = build_program(cfg)
    res = run_bass_kernel_spmd(nc, in_maps, core_ids=list(range(NCORES)))
    out = np.concatenate([r["out"] for r in res.results], axis=0)
    return out.reshape(32, cfg.L, cfg.D)


# version 09
# speedup vs baseline: 1.0560x; 1.0560x over previous
import numpy as np
import ml_dtypes
import concourse.bass as bass
import concourse.mybir as mybir
from concourse.bass_utils import run_bass_kernel_spmd
from contextlib import ExitStack

F32 = mybir.dt.float32
BF16 = mybir.dt.bfloat16
AF = mybir.ActivationFunctionType
ALU = mybir.AluOpType
AX = mybir.AxisListType

NCORES = 8


class Cfg:
    def __init__(self, D=1024, CI=2048, NS=16, RDT=64, L=2048, BL=4, E=8,
                 HID=4096, FF_G=1024, TA=512, TB=256, ftscap=512, fdcap=512):
        self.ftscap, self.fdcap = ftscap, fdcap
        self.D, self.CI, self.NS, self.RDT = D, CI, NS, RDT
        self.L, self.BL, self.E, self.HID = L, BL, E, HID
        self.FF_G, self.TA, self.TB = FF_G, TA, TB
        self.C2 = 2 * CI
        self.T = BL * L
        assert self.T % FF_G == 0 and FF_G % 128 == 0
        assert L % TA == 0 and L % TB == 0 and TA % 128 == 0 and TB % 128 == 0
        assert D % 128 == 0 and CI % 128 == 0 and HID % 128 == 0
        assert HID % 4 == 0 and (HID // 4) % 128 == 0
        assert RDT <= 128 and NS <= 16 and TA <= 512 and TB <= 512


FULL = Cfg()
ENGINES = ("sync", "scalar", "vector", "tensor", "gpsimd")


class Counter:
    __slots__ = ("h", "n")

    def __init__(self, h):
        self.h = h
        self.n = 0


class Prog:
    ENG2SEM = {"tensor": "PE", "scalar": "ACT", "vector": "DVE",
               "gpsimd": "PL", "sync": "SY"}

    def __init__(self, nc):
        self.nc = nc
        self.items = []

    def init_sems(self, ctx):
        e = ctx.enter_context
        self._ctx = ctx
        self.sems = {}
        self.streams = {}
        for name in ("PE", "ACT", "DVE", "PL", "SY"):
            self.sems[name] = Counter(e(self.nc.semaphore(name)))

    def on(self, eng, fn):
        self.items.append((eng, fn))

    def stream(self, name):
        if name not in self.streams:
            self.streams[name] = Counter(
                self._ctx.enter_context(self.nc.semaphore("sm_" + name)))
        return self.streams[name]

    def dma(self, queue, stream, out, in_, **kw):
        sem = self.stream(stream)
        sem.n += 16
        v = sem.n

        def fn(e, out=out, in_=in_, kw=kw, sem=sem):
            e.dma_start(out=out, in_=in_, **kw).then_inc(sem.h, 16)

        self.on(queue, fn)
        return v

    def dwait(self, eng, stream, val=None):
        if stream not in self.streams:
            return
        sem = self.streams[stream]
        v = sem.n if val is None else val
        if v == 0:
            return

        def fn(e, sem=sem, v=v):
            e.wait_ge(sem.h, v)

        self.on(eng, fn)

    def wait(self, eng, sem_name, val):
        if val is None:
            return
        sem = self.sems[sem_name]

        def fn(e, sem=sem, val=val):
            e.wait_ge(sem.h, val)

        self.on(eng, fn)

    def run(self, eng, fn, inc=False, drain=False):
        sem = self.sems[self.ENG2SEM[eng]]
        v = None
        if inc or drain:
            sem.n += 1
            v = sem.n
        if inc:
            def wrapped(e, fn=fn, sem=sem):
                fn(e).then_inc(sem.h, 1)
            self.on(eng, wrapped)
        elif drain:
            def wrapped(e, fn=fn, sem=sem):
                fn(e)
                e.drain().then_inc(sem.h, 1)
            self.on(eng, wrapped)
        else:
            self.on(eng, fn)
        return v

    def drain_inc(self, eng):
        sem = self.sems[self.ENG2SEM[eng]]
        sem.n += 1
        v = sem.n

        def fn(e, sem=sem):
            e.drain().then_inc(sem.h, 1)

        self.on(eng, fn)
        return v

    def barrier(self, engines=ENGINES):
        for s in list(self.streams):
            self.dwait("sync", s)
        vals = {}
        for eng in engines:
            sem = self.sems[self.ENG2SEM[eng]]
            nprev = sem.n
            sem.n += 1
            vals[eng] = sem.n
            if eng in ("vector", "scalar", "gpsimd"):
                def fn(e, sem=sem):
                    e.drain().then_inc(sem.h, 1)
            else:
                def fn(e, sem=sem, nprev=nprev):
                    if nprev:
                        e.wait_ge(sem.h, nprev)
                    e.nop().then_inc(sem.h, 1)
            self.on(eng, fn)
        for eng in engines:
            for other in engines:
                if other != eng:
                    self.wait(eng, self.ENG2SEM[other], vals[other])

    def emit_all(self):
        nc = self.nc
        with nc.Block() as block:
            for eng_name in ENGINES:
                def body(e, eng_name=eng_name):
                    for eng, fn in self.items:
                        if eng == eng_name:
                            fn(e)
                getattr(block, eng_name)(body)


def build_program(cfg: Cfg, debug_outs=False):
    nc = bass.Bass("TRN2", target_bir_lowering=False)
    c = cfg
    KD = c.D // 128
    KC = c.CI // 128
    KH = c.HID // 128
    TPG = c.FF_G // 128
    NG = c.T // c.FF_G
    NCA = c.L // c.TA
    NCB = c.L // c.TB
    LT = c.L // 128
    FTS = min(c.ftscap, c.FF_G)
    NTS = c.FF_G // FTS
    FD = min(c.fdcap, c.D)
    NF = c.D // FD
    HQ = c.HID // 4
    NHT = HQ // 128
    NXP = c.RDT + 2 * c.NS

    def din(name, shape, dt=F32):
        return nc.dram_tensor(name, list(shape), dt, kind="ExternalInput")

    x_d = din("x", (c.T, c.D))
    fwgT = din("fwgT", (c.E, c.D, c.HID))
    fwuT = din("fwuT", (c.E, c.D, c.HID))
    fwdT = din("fwdT", (c.E, c.HID, c.D))
    fgT = din("fgT", (c.D, c.E))
    inwT = [din("inwT1", (c.D, c.C2)), din("inwT2", (c.D, c.C2))]
    outwT = [din("outwT1", (c.CI, c.D)), din("outwT2", (c.CI, c.D))]
    xpwT = [din("xpwT1", (c.CI, NXP)), din("xpwT2", (c.CI, NXP))]
    dtpwT = [din("dtpwT1", (c.RDT, c.CI)), din("dtpwT2", (c.RDT, c.CI))]
    convw = [din("convw1", (128, KC * 4)), din("convw2", (128, KC * 4))]
    convb = [din("convb1", (128, KC)), din("convb2", (128, KC))]
    dtpb = [din("dtpb1", (128, KC)), din("dtpb2", (128, KC))]
    dvec = [din("dvec1", (128, KC)), din("dvec2", (128, KC))]
    alog = [din("alog1", (128, KC * c.NS)), din("alog2", (128, KC * c.NS))]
    ident_d = din("ident", (128, 128), BF16)
    antid_d = din("antid", (128, 128), BF16)
    out_d = nc.dram_tensor("out", [c.T, c.D], F32, kind="ExternalOutput")

    def scr(name, shape, dt):
        return nc.dram_tensor(name, list(shape), dt, kind="Internal")

    bwg = scr("bwg", (c.E, c.D, c.HID), BF16)
    bwu = scr("bwu", (c.E, c.D, c.HID), BF16)
    bwd = scr("bwd", (c.E, c.HID, c.D), BF16)
    bgT = scr("bgT", (c.D, c.E), BF16)
    binw = [scr("binw1", (c.D, c.C2), BF16), scr("binw2", (c.D, c.C2), BF16)]
    boutw = [scr("boutw1", (c.CI, c.D), BF16), scr("boutw2", (c.CI, c.D), BF16)]
    bxpw = [scr("bxpw1", (c.CI, NXP), BF16), scr("bxpw2", (c.CI, NXP), BF16)]
    bdtpw = [scr("bdtpw1", (c.RDT, c.CI), BF16),
             scr("bdtpw2", (c.RDT, c.CI), BF16)]
    dkind = "ExternalOutput" if debug_outs else "Internal"
    x1_d = nc.dram_tensor("x1", [c.T, c.D], F32, kind=dkind)
    x2_d = nc.dram_tensor("x2", [c.T, c.D], F32, kind=dkind)
    xnTm_d = scr("xnTm", (c.D, c.L), BF16)
    xnTr_d = scr("xnTr", (c.D, c.L), BF16)
    sDel = scr("sDel", (c.CI, c.L), F32)
    sDu = scr("sDu", (c.CI, c.L), F32)
    sSz = scr("sSz", (c.CI, c.L), F32)
    sXd = scr("sXd", (c.CI, c.L), F32)
    sB = scr("sB", (c.NS, c.L), F32)
    sC = scr("sC", (c.NS, c.L), F32)
    yg_d = [scr("yg1", (c.CI, c.L), BF16), scr("yg2", (c.CI, c.L), BF16)]

    ctx = ExitStack()
    e_ = ctx.enter_context
    P = Prog(nc)
    P.init_sems(ctx)

    ident = e_(nc.sbuf_tensor("identsb", [128, 128], BF16))
    antid = e_(nc.sbuf_tensor("antidsb", [128, 128], BF16))
    ggT = e_(nc.sbuf_tensor("ggTsb", [128, KD * c.E], BF16))
    A_sb = e_(nc.sbuf_tensor("A_sb", [128, 2 * KC * c.NS], F32))
    cw_sb = e_(nc.sbuf_tensor("cw_sb", [128, 2 * KC * 4], F32))
    cb_sb = e_(nc.sbuf_tensor("cb_sb", [128, 2 * KC], F32))
    db_sb = e_(nc.sbuf_tensor("db_sb", [128, 2 * KC], F32))
    dv_sb = e_(nc.sbuf_tensor("dv_sb", [128, 2 * KC], F32))
    ones1 = e_(nc.sbuf_tensor("ones1", [1, 128], F32))
    stats = e_(nc.sbuf_tensor("stats", [128, 128], F32))
    glog = e_(nc.sbuf_tensor("glog", [128, TPG * c.E], F32))
    gtmp = e_(nc.sbuf_tensor("gtmp", [128, 2 * TPG * c.E], F32))
    gsc = e_(nc.sbuf_tensor("gsc", [128, 8 * TPG], F32))
    comb = e_(nc.sbuf_tensor("comb", [128, TPG * c.E], F32))

    AR_U = nc.sbuf_bytes_remaining // 4 - 512
    arena = e_(nc.sbuf_tensor("arena", [128, AR_U], F32))

    def af32(off, n):
        assert off + n <= AR_U, (off, n, AR_U)
        return arena[:, off:off + n]

    def abf16(off, nf32):
        assert off + nf32 <= AR_U, (off, nf32, AR_U)
        return arena[:, off:off + nf32].bitcast(BF16)

    pb = [e_(nc.psum_tensor(f"pb{i}", [128, 512], F32)) for i in range(6)]
    pT = e_(nc.psum_tensor("pT", [128, KD * 128], BF16))
    pGt = e_(nc.psum_tensor("pGt", [128, 512], F32))

    prep_ready = {}

    def prep():
        P.dma("sync", "cst", ident[:], ident_d.ap())
        P.dma("sync", "cst", antid[:], antid_d.ap())
        for d in range(2):
            P.dma("sync", "cst", cw_sb[:, d * KC * 4:(d + 1) * KC * 4], convw[d].ap())
            P.dma("sync", "cst", cb_sb[:, d * KC:(d + 1) * KC], convb[d].ap())
            P.dma("sync", "cst", db_sb[:, d * KC:(d + 1) * KC], dtpb[d].ap())
            P.dma("sync", "cst", dv_sb[:, d * KC:(d + 1) * KC], dvec[d].ap())
        v_alog = None
        for d in range(2):
            v_alog = P.dma("sync", "cst",
                           A_sb[:, d * KC * c.NS:(d + 1) * KC * c.NS],
                           alog[d].ap())
        P.run("gpsimd", lambda e: e.memset(ones1[:], 1.0))
        order = []
        for ee in range(c.E):
            order += [(bwg, fwgT, ("wg", ee), ee), (bwu, fwuT, ("wu", ee), ee),
                      (bwd, fwdT, ("wd", ee), ee)]
        order.append((bgT, fgT, ("g", 0), None))
        for d in range(2):
            order += [(binw[d], inwT[d], ("inw", d), None),
                      (boutw[d], outwT[d], ("outw", d), None),
                      (bxpw[d], xpwT[d], ("xpw", d), None),
                      (bdtpw[d], dtpwT[d], ("dtpw", d), None)]
        for dst, src, key, ee in order:
            if key == ("g", 0):
                stream = "cg"
            elif ee is None:
                stream = "cm"
            else:
                stream = f"ce{ee}"
            if ee is None:
                P.dma("gpsimd", stream, dst.ap(), src.ap())
            else:
                P.dma("gpsimd", stream, dst.ap()[ee], src.ap()[ee])
        P.dwait("sync", "cg")
        P.dma("sync", "cst", ggT[:].rearrange("p (k e) -> p k e", e=c.E),
              bgT.ap().rearrange("(k p) e -> p k e", p=128))
        P.dwait("scalar", "cst")
        va = P.run("scalar", lambda e: e.activation(
            A_sb[:], A_sb[:], AF.Exp), drain=True)
        P.wait("vector", "ACT", va)
        P.run("vector", lambda e: e.tensor_scalar_mul(
            A_sb[:], A_sb[:], -1.0), drain=True)

    def ln_tiles(src_d, t0, ntiles, xs_off, consume):
        nD = KD * 128
        s1, s2 = stats[:, 0:16], stats[:, 16:32]
        mm_, vv = stats[:, 32:48], stats[:, 48:64]
        rstd, nmr = stats[:, 64:80], stats[:, 80:96]
        sq = af32(xs_off + 3 * nD, nD)
        rd_v = [None, None]
        sq_v = [None, None]
        va = None
        for j in range(ntiles):
            sl = af32(xs_off + (j % 2) * nD, nD)
            xstream = "xa" if j % 2 == 0 else "xb"
            P.wait("sync", "DVE", rd_v[j % 2])
            P.wait("sync", "ACT", sq_v[j % 2])
            P.dma("sync", xstream, sl,
                  src_d.ap()[t0 + j * 128:t0 + (j + 1) * 128, :])
            P.dwait("vector", xstream)
            rd_v[j % 2] = P.run("vector", lambda e, sl=sl, j=j: e.reduce_sum(
                s1[:, j:j + 1], sl, axis=AX.X), inc=True)
            P.dwait("scalar", xstream)
            P.wait("scalar", "ACT", sq_v[(j - 1) % 2] if j >= 1 else None)
            sq_v[j % 2] = P.run("scalar", lambda e, sl=sl, j=j: e.activation(
                sq, sl, AF.Square, accum_out=s2[:, j:j + 1]), inc=True)
        va = P.drain_inc("scalar")
        vd0 = P.drain_inc("vector")
        P.wait("vector", "ACT", va)
        P.run("vector", lambda e: e.tensor_scalar_mul(
            mm_[:, 0:ntiles], s1[:, 0:ntiles], 1.0 / nD))
        P.run("vector", lambda e: e.tensor_scalar_mul(
            vv[:, 0:ntiles], s2[:, 0:ntiles], 1.0 / nD))
        P.run("vector", lambda e: e.drain())
        P.run("vector", lambda e: e.tensor_tensor(
            s1[:, 0:ntiles], mm_[:, 0:ntiles], mm_[:, 0:ntiles], ALU.mult))
        P.run("vector", lambda e: e.drain())
        P.run("vector", lambda e: e.tensor_sub(
            vv[:, 0:ntiles], vv[:, 0:ntiles], s1[:, 0:ntiles]))
        P.run("vector", lambda e: e.drain())
        vd = P.run("vector", lambda e: e.tensor_scalar_add(
            vv[:, 0:ntiles], vv[:, 0:ntiles], 1e-5), drain=True)
        P.wait("scalar", "DVE", vd)
        va = P.run("scalar", lambda e: e.activation(
            rstd[:, 0:ntiles], vv[:, 0:ntiles], AF.Ln), drain=True)
        P.wait("vector", "ACT", va)
        P.run("vector", lambda e: e.tensor_scalar_mul(
            rstd[:, 0:ntiles], rstd[:, 0:ntiles], -0.5))
        vd = P.drain_inc("vector")
        P.wait("scalar", "DVE", vd)
        va = P.run("scalar", lambda e: e.activation(
            rstd[:, 0:ntiles], rstd[:, 0:ntiles], AF.Exp), drain=True)
        P.wait("vector", "ACT", va)
        P.run("vector", lambda e: e.tensor_mul(
            nmr[:, 0:ntiles], mm_[:, 0:ntiles], rstd[:, 0:ntiles]))
        P.run("vector", lambda e: e.drain())
        vd = P.run("vector", lambda e: e.tensor_scalar_mul(
            nmr[:, 0:ntiles], nmr[:, 0:ntiles], -1.0), drain=True)
        P.wait("scalar", "DVE", vd)
        xn_v = [None, None]
        xread_v = [vd, vd]
        P.wait("sync", "ACT", va)
        P.wait("sync", "DVE", vd)
        xread_v = [None, None]
        for j in range(ntiles):
            sl = af32(xs_off + (j % 2) * nD, nD)
            xstream = "xa" if j % 2 == 0 else "xb"
            xnsl = abf16(xs_off + 2 * nD + (j % 2) * (nD // 2), nD // 2)
            P.wait("sync", "ACT", xread_v[j % 2])
            P.wait("sync", "PE", xn_v[j % 2])
            P.dma("sync", xstream, sl,
                  src_d.ap()[t0 + j * 128:t0 + (j + 1) * 128, :])
            P.dwait("scalar", xstream)
            xread_v[j % 2] = P.run(
                "scalar", lambda e, sl=sl, xnsl=xnsl, j=j: e.activation(
                    xnsl, sl, AF.Identity, bias=nmr[:, j:j + 1],
                    scale=rstd[:, j:j + 1]), drain=True)
            P.wait("tensor", "ACT", xread_v[j % 2])
            xn_v[j % 2] = consume(j, xnsl)

    def emit_gate():
        n = TPG * c.E
        lv = glog[:, 0:n].rearrange("p (j e) -> p j e", e=c.E)
        m1 = gsc[:, 0:TPG]
        m2 = gsc[:, TPG:2 * TPG]
        ed = gsc[:, 2 * TPG:3 * TPG]
        p1h = gsc[:, 3 * TPG:4 * TPG]
        p2h = gsc[:, 4 * TPG:5 * TPG]
        t0v = gtmp[:, 0:n].rearrange("p (j e) -> p j e", e=c.E)
        t1v = gtmp[:, n:2 * n].rearrange("p (j e) -> p j e", e=c.E)
        b_ = lambda a: a[:, :, None].broadcast_to([128, TPG, c.E])
        P.run("vector", lambda e: e.reduce_max(m1[:, 0:TPG], lv, axis=AX.X))
        P.run("vector", lambda e: e.drain())
        P.run("vector", lambda e: e.tensor_tensor(t0v, lv, b_(m1), ALU.is_ge))
        P.run("vector", lambda e: e.drain())
        P.run("vector", lambda e: e.scalar_tensor_tensor(
            t1v, t0v, -1e30, lv, ALU.mult, ALU.add))
        P.run("vector", lambda e: e.drain())
        P.run("vector", lambda e: e.reduce_max(m2[:, 0:TPG], t1v, axis=AX.X))
        P.run("vector", lambda e: e.drain())
        P.run("vector", lambda e: e.tensor_tensor(t1v, t1v, b_(m2), ALU.is_ge))
        P.run("vector", lambda e: e.drain())
        vd = P.run("vector", lambda e: e.tensor_sub(
            m2[:, 0:TPG], m2[:, 0:TPG], m1[:, 0:TPG]), drain=True)
        P.wait("scalar", "DVE", vd)
        va = P.run("scalar", lambda e: e.activation(
            ed[:, 0:TPG], m2[:, 0:TPG], AF.Exp), drain=True)
        P.wait("vector", "ACT", va)
        P.run("vector", lambda e: e.tensor_scalar_add(
            p1h[:, 0:TPG], ed[:, 0:TPG], 1.0))
        P.run("vector", lambda e: e.drain())
        P.run("vector", lambda e: e.reciprocal(p1h[:, 0:TPG], p1h[:, 0:TPG]))
        P.run("vector", lambda e: e.drain())
        P.run("vector", lambda e: e.tensor_mul(
            p2h[:, 0:TPG], ed[:, 0:TPG], p1h[:, 0:TPG]))
        P.run("vector", lambda e: e.drain())
        P.run("vector", lambda e: e.tensor_scalar_mul(
            p1h[:, 0:TPG], p1h[:, 0:TPG], 0.5))
        P.run("vector", lambda e: e.tensor_scalar_mul(
            p2h[:, 0:TPG], p2h[:, 0:TPG], 0.5))
        P.run("vector", lambda e: e.drain())
        cv = comb[:, 0:n].rearrange("p (j e) -> p j e", e=c.E)
        P.run("vector", lambda e: e.tensor_mul(t0v, t0v, b_(p1h)))
        P.run("vector", lambda e: e.tensor_mul(t1v, t1v, b_(p2h)))
        P.run("vector", lambda e: e.drain())
        return P.run("vector", lambda e: e.tensor_add(cv, t0v, t1v), drain=True)

    u_xnt = KD * c.FF_G // 2
    u_acc = TPG * c.D
    u_hid = KH * c.FF_G // 2
    u_qp = 2 * (KD * HQ) // 2
    u_w = max(2 * u_qp, KH * c.D // 2)
    u_xs = 4 * KD * 128
    FF_XNT = 0
    FF_ACC = FF_XNT + u_xnt
    FF_HID = FF_ACC + u_acc
    FF_W = FF_HID + u_hid
    FF_XS = FF_W + u_w
    FF_SIL = FF_XS + u_xs
    u_sil = 2 * FTS
    assert FF_SIL + u_sil <= AR_U

    state = {"g_ev": [None, None], "u_ev": [None, None], "o_ev": [None, None],
             "m3_last": None, "hid_last": None, "gate_act": None}

    def ffn_call(call):
        src_d = x_d if call == 0 else x2_d
        dst_d = x1_d if call == 0 else out_d
        xnT = abf16(FF_XNT, u_xnt)
        hid = abf16(FF_HID, u_hid)
        acc = af32(FF_ACC, u_acc)
        wq = [abf16(FF_W + s * u_qp, u_qp) for s in range(2)]
        wd_all = abf16(FF_W, KH * c.D // 2)
        xnTv = xnT.rearrange("p (k t) -> p k t", t=c.FF_G)
        hidv = hid.rearrange("p (kh t) -> p kh t", t=c.FF_G)
        wdv = wd_all.rearrange("p (k d) -> p k d", k=KH)
        accv = acc.rearrange("p (m d) -> p m d", m=TPG)
        for g in range(NG):
            t0 = g * c.FF_G
            last_xnt = [None]

            def consume(j, xnsl):
                def tfn(e, xnsl=xnsl):
                    for k in range(KD):
                        ins = e.transpose(
                            pT[:, k * 128:(k + 1) * 128],
                            xnsl.rearrange("p (k t) -> p k t", k=KD)[:, k],
                            ident[:])
                    return ins
                vp = P.run("tensor", tfn, inc=True)
                P.wait("vector", "PE", vp)
                vd = P.run("vector", lambda e, j=j: e.tensor_copy(
                    xnTv[:, :, j * 128:(j + 1) * 128],
                    pT[:, 0:KD * 128].rearrange("p (k t) -> p k t", k=KD)),
                    drain=True)
                last_xnt[0] = vd
                P.wait("tensor", "DVE", vd)

                def gmm(e, j=j):
                    for k in range(KD):
                        ins = e.matmul(
                            pGt[:, 0:c.E],
                            xnTv[:, k, j * 128:(j + 1) * 128],
                            ggT[:].rearrange("p (k e) -> p k e", e=c.E)[:, k],
                            start=(k == 0), stop=(k == KD - 1))
                    return ins
                vp2 = P.run("tensor", gmm, inc=True)
                P.wait("scalar", "PE", vp2)
                va = P.run("scalar", lambda e, j=j: e.copy(
                    glog[:, j * c.E:(j + 1) * c.E], pGt[:, 0:c.E]), drain=True)
                state["gate_act"] = va
                P.wait("tensor", "ACT", va)
                return vp2

            ln_tiles(src_d, t0, TPG, FF_XS, consume)
            P.wait("vector", "ACT", state["gate_act"])
            emit_gate()
            for ee in range(c.E):
                P.wait("sync", "PE", state["m3_last"])
                P.dwait("sync", f"ce{ee}")
                qv = [None] * 4
                pe_q = [None] * 4

                def load_quarter(q):
                    sl = wq[q % 2]
                    ws = "wqa" if q % 2 == 0 else "wqb"
                    slv = sl.rearrange("p (m k h) -> p m k h", m=2, k=KD)
                    for k in range(KD):
                        P.dma("sync", ws, slv[:, 0, k],
                              bwg.ap()[ee, k * 128:(k + 1) * 128,
                                       q * HQ:(q + 1) * HQ])
                    for k in range(KD):
                        P.dma("sync", ws, slv[:, 1, k],
                              bwu.ap()[ee, k * 128:(k + 1) * 128,
                                       q * HQ:(q + 1) * HQ])

                load_quarter(0)
                load_quarter(1)
                P.wait("vector", "PE", state["m3_last"])
                ht_glob = 0
                for q in range(4):
                    P.dwait("tensor", "wqa" if q % 2 == 0 else "wqb")
                    if q == 0:
                        P.wait("tensor", "DVE", last_xnt[0])
                    slv = wq[q % 2].rearrange("p (m k h) -> p m k h",
                                              m=2, k=KD)
                    for ht in range(NHT):
                        kidx = q * NHT + ht
                        for ts_ in range(NTS):
                            sg = ht_glob % 2
                            pg, pu = pb[sg], pb[2 + sg]
                            P.wait("tensor", "ACT", state["g_ev"][sg])
                            P.wait("tensor", "DVE", state["u_ev"][sg])

                            def m1(e, slv=slv, ht=ht, ts_=ts_, pg=pg):
                                for k in range(KD):
                                    ins = e.matmul(
                                        pg[:, 0:FTS],
                                        slv[:, 0, k, ht * 128:(ht + 1) * 128],
                                        xnTv[:, k, ts_ * FTS:(ts_ + 1) * FTS],
                                        start=(k == 0), stop=(k == KD - 1))
                                return ins
                            v1 = P.run("tensor", m1, inc=True)

                            def m2(e, slv=slv, ht=ht, ts_=ts_, pu=pu):
                                for k in range(KD):
                                    ins = e.matmul(
                                        pu[:, 0:FTS],
                                        slv[:, 1, k, ht * 128:(ht + 1) * 128],
                                        xnTv[:, k, ts_ * FTS:(ts_ + 1) * FTS],
                                        start=(k == 0), stop=(k == KD - 1))
                                return ins
                            v2 = P.run("tensor", m2, inc=True)
                            pe_q[q] = v2
                            P.wait("scalar", "PE", v1)
                            sil = af32(FF_SIL + sg * FTS, FTS)
                            va = P.run("scalar",
                                       lambda e, pg=pg, sil=sil: e.activation(
                                           sil, pg[:, 0:FTS], AF.Silu),
                                       inc=True)
                            state["g_ev"][sg] = va
                            P.wait("vector", "ACT", va)
                            P.wait("vector", "PE", v2)
                            hcols = hidv[:, kidx, ts_ * FTS:(ts_ + 1) * FTS]
                            vd = P.run("vector",
                                       lambda e, sil=sil, pu=pu, hcols=hcols:
                                       e.tensor_tensor(hcols, sil,
                                                       pu[:, 0:FTS],
                                                       ALU.mult), inc=True)
                            state["u_ev"][sg] = vd
                            ht_glob += 1
                    if q + 2 <= 3:
                        P.wait("sync", "PE", pe_q[q])
                        load_quarter(q + 2)
                state["hid_last"] = P.drain_inc("vector")
                P.wait("sync", "PE", pe_q[3])
                for k in range(KH):
                    P.dma("sync", "wdl", wdv[:, k],
                          bwd.ap()[ee, k * 128:(k + 1) * 128, :])
                P.dwait("tensor", "wdl")
                P.wait("tensor", "DVE", state["hid_last"])
                for m in range(TPG):
                    for f in range(NF):
                        so = (m * NF + f) % 2
                        po = pb[4 + so]
                        P.wait("tensor", "DVE", state["o_ev"][so])

                        def m3(e, m=m, f=f, po=po):
                            for k in range(KH):
                                ins = e.matmul(
                                    po[:, 0:FD],
                                    hidv[:, k, m * 128:(m + 1) * 128],
                                    wdv[:, k, f * FD:(f + 1) * FD],
                                    start=(k == 0), stop=(k == KH - 1))
                            return ins
                        vp = P.run("tensor", m3, inc=True)
                        state["m3_last"] = vp
                        P.wait("vector", "PE", vp)
                        accsl = accv[:, m, f * FD:(f + 1) * FD]
                        ccol = comb[:, m * c.E + ee:m * c.E + ee + 1]
                        if ee == 0:
                            vd = P.run("vector",
                                       lambda e, accsl=accsl, po=po,
                                       ccol=ccol: e.tensor_scalar(
                                           accsl, po[:, 0:FD], ccol, None,
                                           ALU.mult), inc=True)
                        else:
                            vd = P.run("vector",
                                       lambda e, accsl=accsl, po=po,
                                       ccol=ccol: e.scalar_tensor_tensor(
                                           accsl, po[:, 0:FD], ccol, accsl,
                                           ALU.mult, ALU.add), inc=True)
                        state["o_ev"][so] = vd
            vacc = P.drain_inc("vector")
            fin_rd = [vacc, vacc]
            for j in range(TPG):
                sl = af32(FF_XS + (j % 2) * KD * 128, KD * 128)
                xstream = "xa" if j % 2 == 0 else "xb"
                P.wait("sync", "DVE", fin_rd[j % 2])
                P.dwait("sync", "sfo")
                P.dma("sync", xstream, sl,
                      src_d.ap()[t0 + j * 128:t0 + (j + 1) * 128, :])
                P.dwait("vector", xstream)
                vd = P.run("vector", lambda e, sl=sl, j=j: e.tensor_add(
                    sl, sl, accv[:, j]), drain=True)
                fin_rd[j % 2] = vd
                P.wait("gpsimd", "DVE", vd)
                P.dma(
                    "gpsimd", "sfo",
                    dst_d.ap()[t0 + j * 128:t0 + (j + 1) * 128, :], sl)
            P.barrier()

    u_xi = KC * (c.TA + 3)
    u_xc = KC * c.TA
    u_xcb = KC * c.TA // 2
    u_xnt_m = KD * c.TA // 2
    u_inw = KD * c.C2 // 2
    u_xpw = (KC * NXP + 1) // 2
    u_dtpw = c.CI // 2
    u_dtbf = c.TA // 2
    u_stg = 10 * c.TA
    MA_XI = 0
    MA_XC = MA_XI + u_xi
    MA_XCB = MA_XC + u_xc
    MA_XNT = MA_XCB + u_xcb
    MA_INW = MA_XNT + u_xnt_m
    MA_XPW = MA_INW + u_inw
    MA_DTPW = MA_XPW + u_xpw
    MA_DTBF = MA_DTPW + u_dtpw
    MA_STG = MA_DTBF + u_dtbf
    MA_BST = MA_STG + u_stg
    assert MA_BST + 2 * c.TA <= AR_U

    u_mb = KC * c.TB
    u_br = c.NS * c.TB
    MB_DEL = 0
    MB_DU = MB_DEL + u_mb
    MB_SZ = MB_DU + u_mb
    MB_XD = MB_SZ + u_mb
    MB_BR = MB_XD + u_mb
    MB_CR = MB_BR + u_br
    MB_S = MB_CR + u_br
    MB_DA = MB_S + 2 * u_br
    MB_DB = MB_DA + u_br
    MB_YTF = MB_DB + 2 * u_br
    MB_YGF = MB_DA
    MB_H = MB_YTF + u_mb
    assert MB_H + KC * c.NS <= AR_U

    u_ygm = 2 * (KC * 128 // 2)
    u_ow = 2 * (KC * c.D // 2)
    MC_YG = 0
    MC_OW = MC_YG + u_ygm
    MC_X1 = MC_OW + u_ow
    MC_X2 = MC_X1 + 2 * KD * 128
    assert MC_X2 + 2 * KD * 128 <= AR_U

    def mamba_batch(b):
        stg = abf16(MA_STG, KD * 128)
        stgv = stg.rearrange("p (r k t) -> p r k t", r=2, k=KD)
        st_v = [[None, None], [None, None]]

        def consume(j, xnsl):
            vp_last = None
            for rev in range(2):
                idm = antid if rev else ident

                def tfn(e, xnsl=xnsl, idm=idm):
                    for k in range(KD):
                        ins = e.transpose(
                            pT[:, k * 128:(k + 1) * 128],
                            xnsl.rearrange("p (k t) -> p k t", k=KD)[:, k],
                            idm[:])
                    return ins
                vp = P.run("tensor", tfn, inc=True)
                vp_last = vp
                P.wait("vector", "PE", vp)
                P.dwait("vector", "slnm" if rev == 0 else "slnr")
                dst = stgv[:, rev]
                vd = P.run("vector", lambda e, dst=dst: e.tensor_copy(
                    dst, pT[:, 0:KD * 128].rearrange("p (k t) -> p k t",
                                                     k=KD)), drain=True)
                P.wait("gpsimd", "DVE", vd)
                jj = j if rev == 0 else (LT - 1 - j)
                tgt = xnTm_d if rev == 0 else xnTr_d
                P.dma(
                    "gpsimd", "slnm" if rev == 0 else "slnr",
                    tgt.ap().rearrange("(k p) t -> p k t", p=128)
                       [:, :, jj * 128:(jj + 1) * 128],
                    dst)
                P.wait("tensor", "DVE", vd)
            return vp_last

        ln_tiles(x1_d, b * c.L, LT, MA_STG + KD * 128, consume)
        P.barrier()
        for dirn in range(2):
            mamba_front(b, dirn)
            P.barrier()
            mamba_scan(b, dirn)
            P.barrier()
        mamba_out(b)
        P.barrier()

    def mamba_front(b, dirn):
        xnsrc = xnTm_d if dirn == 0 else xnTr_d
        inw = abf16(MA_INW, u_inw)
        inwv = inw.rearrange("p (k m) -> p k m", k=KD)
        dtpw = abf16(MA_DTPW, u_dtpw)
        xi = af32(MA_XI, u_xi)
        xiv = xi.rearrange("p (ct t) -> p ct t", ct=KC)
        xc = af32(MA_XC, u_xc)
        xcv = xc.rearrange("p (ct t) -> p ct t", ct=KC)
        xcb = abf16(MA_XCB, u_xcb)
        xcbv = xcb.rearrange("p (k t) -> p k t", k=KC)
        dtbf = abf16(MA_DTBF, u_dtbf)
        P.dwait("sync", "cm")
        for k in range(KD):
            P.dma("sync", "mw", inwv[:, k],
                  binw[dirn].ap()[k * 128:(k + 1) * 128, :])
        xpw_full = abf16(MA_XPW, u_xpw)
        for k in range(KC):
            P.dma("sync", "mw",
                  xpw_full[:, k * NXP:(k + 1) * NXP],
                  bxpw[dirn].ap()[k * 128:(k + 1) * 128, :])
        P.dma("sync", "mw", dtpw[0:c.RDT, 0:c.CI], bdtpw[dirn].ap())
        vz = P.run("gpsimd", lambda e: e.memset(xiv[:, :, 0:3], 0.0),
                   drain=True)
        P.wait("scalar", "PL", vz)
        pe_ev = [None, None]
        dtp_ev = [None, None]
        sz_st = [None, None]
        xd_st = [None, None]
        dl_st = [None, None]
        du_st = [None, None]
        du_rd = [None, None]
        bc_st = None
        prev_vcv = None
        xz_last = None
        xp_last = None
        for ch in range(NCA):
            tt0 = ch * c.TA
            xnt = abf16(MA_XNT, u_xnt_m)
            xntv = xnt.rearrange("p (k t) -> p k t", k=KD)
            P.wait("sync", "PE", xz_last)
            for k in range(KD):
                P.dma("sync", "xnt", xntv[:, k],
                      xnsrc.ap()[k * 128:(k + 1) * 128, tt0:tt0 + c.TA])
            P.dwait("tensor", "xnt")
            if ch == 0:
                P.dwait("tensor", "mw")
            if ch > 0:
                P.wait("scalar", "DVE", prev_vcv)
                P.run("scalar", lambda e: e.copy(
                    xiv[:, :, 0:3], xiv[:, :, c.TA:c.TA + 3]))
                P.run("scalar", lambda e: e.drain())
            for m in range(2 * KC):
                sg = m % 2
                pg = pb[sg]
                g1 = max(pe_ev[sg] or 0, (dtp_ev[0] or 0) if sg == 0 else 0)
                P.wait("tensor", "ACT", g1 if g1 else None)

                def mmx(e, m=m, pg=pg):
                    for k in range(KD):
                        ins = e.matmul(
                            pg[:, 0:c.TA],
                            inwv[:, k, m * 128:(m + 1) * 128],
                            xntv[:, k],
                            start=(k == 0), stop=(k == KD - 1))
                    return ins
                vp = P.run("tensor", mmx, inc=True)
                xz_last = vp
                P.wait("scalar", "PE", vp)
                if m < KC:
                    dst = xiv[:, m, 3:3 + c.TA]
                    va = P.run("scalar", lambda e, dst=dst, pg=pg:
                               e.copy(dst, pg[:, 0:c.TA]), inc=True)
                else:
                    P.dwait("scalar", "ssz")
                    szst = af32(MA_STG + 6 * c.TA + sg * c.TA, c.TA)
                    va = P.run("scalar", lambda e, szst=szst, pg=pg:
                               e.activation(szst, pg[:, 0:c.TA], AF.Silu),
                               drain=True)
                    P.wait("gpsimd", "ACT", va)
                    P.dma(
                        "gpsimd", "ssz",
                        sSz.ap()[(m - KC) * 128:(m - KC + 1) * 128,
                                 tt0:tt0 + c.TA], szst)
                pe_ev[sg] = va
            vxi = P.drain_inc("scalar")
            P.wait("vector", "ACT", vxi)
            for ct in range(KC):
                xitile = xiv[:, ct]
                xct = xcv[:, ct]
                cwc = cw_sb[:, dirn * KC * 4 + ct * 4:dirn * KC * 4 + ct * 4 + 4]

                def conv(e, xitile=xitile, xct=xct, cwc=cwc):
                    e.tensor_scalar(xct, xitile[:, 0:c.TA], cwc[:, 0:1],
                                    None, ALU.mult)
                    ins = None
                    for jj in range(1, 4):
                        e.drain()
                        ins = e.scalar_tensor_tensor(
                            xct, xitile[:, jj:jj + c.TA], cwc[:, jj:jj + 1],
                            xct, ALU.mult, ALU.add)
                    return ins
                P.run("vector", conv)
            vcv = P.drain_inc("vector")
            prev_vcv = vcv
            P.wait("scalar", "DVE", vcv)
            for ct in range(KC):
                xct = xcv[:, ct]
                P.run("scalar", lambda e, xct=xct, ct=ct: e.activation(
                    xct, xct, AF.Silu,
                    bias=cb_sb[:, dirn * KC + ct:dirn * KC + ct + 1]))
                P.run("scalar", lambda e: e.drain())
                P.dwait("scalar", "sxd")
                xdst = af32(MA_STG + 8 * c.TA + (ct % 2) * c.TA, c.TA)
                va = P.run("scalar", lambda e, xct=xct, ct=ct, xdst=xdst:
                           e.activation(xdst, xct, AF.Identity,
                                        scale=dv_sb[:, dirn * KC + ct:
                                                    dirn * KC + ct + 1]),
                           drain=True)
                P.wait("gpsimd", "ACT", va)
                P.dma(
                    "gpsimd", "sxd", sXd.ap()[ct * 128:(ct + 1) * 128,
                                              tt0:tt0 + c.TA], xdst)
            vsl = P.drain_inc("scalar")
            P.wait("vector", "ACT", vsl)
            P.wait("vector", "PE", xp_last)
            vcb = P.run("vector", lambda e: e.tensor_copy(xcb, xc),
                        drain=True)
            P.wait("tensor", "DVE", vcb)
            parts = [(0, c.RDT, pb[2]), (c.RDT, c.NS, pb[3]),
                     (c.RDT + c.NS, c.NS, pb[5])]
            vp = None
            for (off, mlen, pp) in parts:
                def mmxp(e, off=off, mlen=mlen, pp=pp):
                    ins = None
                    for k in range(KC):
                        ins = e.matmul(
                            pp[0:mlen, 0:c.TA],
                            xpw_full[:, k * NXP + off:k * NXP + off + mlen],
                            xcbv[:, k],
                            start=(k == 0), stop=(k == KC - 1))
                    return ins
                vp = P.run("tensor", mmxp, inc=True)
            xp_last = vp
            P.wait("scalar", "PE", vp)
            P.dwait("scalar", "sbc")
            va = P.run("scalar", lambda e: e.copy(
                dtbf[0:c.RDT, 0:c.TA], pb[2][0:c.RDT, 0:c.TA]), drain=True)
            bst = af32(MA_BST, c.TA)
            cst = af32(MA_BST + c.TA, c.TA)
            P.run("scalar", lambda e, bst=bst: e.copy(
                bst[0:c.NS, :], pb[3][0:c.NS, 0:c.TA]))
            P.run("scalar", lambda e: e.drain())
            vc = P.run("scalar", lambda e, cst=cst: e.copy(
                cst[0:c.NS, :], pb[5][0:c.NS, 0:c.TA]), drain=True)
            P.wait("gpsimd", "ACT", vc)
            P.dma("gpsimd", "sbc", sB.ap()[:, tt0:tt0 + c.TA],
                  bst[0:c.NS, :])
            P.dma("gpsimd", "sbc", sC.ap()[:, tt0:tt0 + c.TA],
                  cst[0:c.NS, :])
            P.wait("tensor", "ACT", va)
            for m in range(KC):
                sg = m % 2
                pp = pb[sg * 4]
                P.wait("tensor", "ACT", dtp_ev[sg])

                def mmd(e, m=m, pp=pp):
                    return e.matmul(
                        pp[:, 0:c.TA],
                        dtpw[0:c.RDT, m * 128:(m + 1) * 128],
                        dtbf[0:c.RDT, 0:c.TA],
                        start=True, stop=True)
                vp = P.run("tensor", mmd, inc=True)
                P.wait("scalar", "PE", vp)
                est = af32(MA_STG + sg * c.TA, c.TA)
                dst_ = af32(MA_STG + 2 * c.TA + sg * c.TA, c.TA)
                dust = af32(MA_STG + 4 * c.TA + sg * c.TA, c.TA)
                P.dwait("scalar", "sdl")
                P.wait("scalar", "DVE", du_rd[sg])
                P.run("scalar", lambda e, est=est, pp=pp, m=m: e.activation(
                    est, pp[:, 0:c.TA], AF.Exp,
                    bias=db_sb[:, dirn * KC + m:dirn * KC + m + 1]))
                P.run("scalar", lambda e: e.drain())
                va = P.run("scalar", lambda e, est=est, dst_=dst_:
                           e.activation(dst_, est, AF.Ln, bias=1.0),
                           drain=True)
                dtp_ev[sg] = va
                P.wait("gpsimd", "ACT", va)
                P.dma(
                    "gpsimd", "sdl", sDel.ap()[m * 128:(m + 1) * 128,
                                               tt0:tt0 + c.TA], dst_)
                P.wait("vector", "ACT", va)
                P.dwait("vector", "sdu")
                xct = xcv[:, m]
                vd = P.run("vector", lambda e, dst_=dst_, dust=dust, xct=xct:
                           e.tensor_mul(dust, dst_, xct), drain=True)
                du_rd[sg] = vd
                P.wait("gpsimd", "DVE", vd)
                P.dma(
                    "gpsimd", "sdu", sDu.ap()[m * 128:(m + 1) * 128,
                                              tt0:tt0 + c.TA], dust)

    def mamba_scan(b, dirn):
        for s in ("ssz", "sxd", "sdl", "sdu", "sbc"):
            P.dwait("sync", s)
        delS = af32(MB_DEL, u_mb)
        delv = delS.rearrange("p (ct t) -> p ct t", ct=KC)
        duS = af32(MB_DU, u_mb)
        duv = duS.rearrange("p (ct t) -> p ct t", ct=KC)
        szS = af32(MB_SZ, u_mb)
        xdS = af32(MB_XD, u_mb)
        brS = af32(MB_BR, u_br)
        crS = af32(MB_CR, u_br)
        daF = af32(MB_DA, u_br)
        ytF = af32(MB_YTF, u_mb)
        ygF = abf16(MB_YGF, u_mb // 2)
        hst = af32(MB_H, KC * c.NS)
        v_scan = {}
        v_red = {}
        g_dbu = {}
        g_mul = {}
        a_da = {}
        a_hc = {}
        prev_vd = None
        NCT = NCB * KC

        def S_of(g):
            return af32(MB_S + (g % 2) * u_br, u_br)

        def db_of(g):
            return af32(MB_DB + (g % 2) * u_br, u_br)

        for cb_ in range(NCB):
            tt0 = cb_ * c.TB
            if cb_ > 0:
                g0 = (cb_ - 1) * KC
                P.wait("sync", "ACT", a_da.get(g0 + KC - 1))
                P.wait("sync", "DVE", prev_vd)
            P.dma("sync", "mbl", delv,
                  sDel.ap().rearrange("(ct p) t -> p ct t", p=128)
                      [:, :, tt0:tt0 + c.TB])
            P.dma("sync", "mbl", duv,
                  sDu.ap().rearrange("(ct p) t -> p ct t", p=128)
                      [:, :, tt0:tt0 + c.TB])
            P.dma("sync", "mbl", szS.rearrange("p (ct t) -> p ct t", ct=KC),
                  sSz.ap().rearrange("(ct p) t -> p ct t", p=128)
                      [:, :, tt0:tt0 + c.TB])
            P.dma("sync", "mbl", xdS.rearrange("p (ct t) -> p ct t", ct=KC),
                  sXd.ap().rearrange("(ct p) t -> p ct t", p=128)
                      [:, :, tt0:tt0 + c.TB])
            P.dma("sync", "mbl",
                  brS.rearrange("p (n t) -> p n t", n=c.NS),
                  sB.ap()[:, tt0:tt0 + c.TB].unsqueeze(0)
                     .broadcast_to([128, c.NS, c.TB]))
            P.dma("sync", "mbl",
                  crS.rearrange("p (n t) -> p n t", n=c.NS),
                  sC.ap()[:, tt0:tt0 + c.TB].unsqueeze(0)
                     .broadcast_to([128, c.NS, c.TB]))
            P.dwait("gpsimd", "mbl")
            P.dwait("scalar", "mbl")
            P.dwait("vector", "mbl")
            if cb_ == 0:
                P.dwait("scalar", "syg")
            for ct in range(KC):
                g = cb_ * KC + ct
                delT = delv[:, ct]
                duT = duv[:, ct]
                S_ = S_of(g)
                Sv = S_.rearrange("p (n t) -> p n t", n=c.NS)
                dbF = db_of(g)
                P.wait("scalar", "DVE", v_scan.get(g - 1))
                va = None
                for n in range(c.NS):
                    acol = A_sb[:, dirn * KC * c.NS + ct * c.NS + n:
                                dirn * KC * c.NS + ct * c.NS + n + 1]
                    va = P.run("scalar", lambda e, delT=delT, n=n, acol=acol:
                               e.activation(
                                   daF[:, n * c.TB:(n + 1) * c.TB], delT,
                                   AF.Exp, scale=acol),
                               inc=(n == c.NS - 1))
                a_da[g] = va
                vb = P.run("vector", lambda e, duT=duT, dbF=dbF:
                           e.tensor_tensor(
                               dbF.rearrange("p (n t) -> p n t", n=c.NS),
                               duT[:, None, :].broadcast_to(
                                   [128, c.NS, c.TB]),
                               brS.rearrange("p (n t) -> p n t", n=c.NS),
                               ALU.mult), inc=True)
                P.wait("scalar", "DVE", vb)
                P.wait("scalar", "ACT", P.sems["ACT"].n)
                ve = P.run("scalar", lambda e: e.nop(), inc=True)
                P.wait("vector", "ACT", ve)
                P.wait("vector", "ACT", a_hc.get(g - 2))
                for n in range(c.NS):
                    init = 0.0 if cb_ == 0 else hst[:, ct * c.NS + n:
                                                    ct * c.NS + n + 1]
                    v_scan[g] = P.run(
                        "vector", lambda e,
                        so=S_[:, n * c.TB:(n + 1) * c.TB],
                        da_=daF[:, n * c.TB:(n + 1) * c.TB],
                        db_=dbF[:, n * c.TB:(n + 1) * c.TB],
                        init=init: e.tensor_tensor_scan(
                            so, da_, db_, init, ALU.mult, ALU.add),
                        inc=(n == c.NS - 1)) or v_scan.get(g)
                P.wait("scalar", "DVE", v_scan[g])
                a_hc[g] = P.run("scalar", lambda e, Sv=Sv, ct=ct: e.copy(
                    hst[:, ct * c.NS:(ct + 1) * c.NS],
                    Sv[:, :, c.TB - 1]), inc=True)
                P.wait("vector", "ACT", a_hc[g])
                vm = P.run("vector", lambda e, S_=S_, dbF=dbF:
                           e.tensor_mul(dbF, S_, crS), inc=True)
                P.wait("scalar", "DVE", vm)
                P.wait("scalar", "ACT", P.sems["ACT"].n)
                ve2 = P.run("scalar", lambda e: e.nop(), inc=True)
                P.wait("vector", "ACT", ve2)
                v_red[g] = P.run("vector", lambda e, dbF=dbF, ct=ct:
                                 e.reduce_sum(
                                     ytF.rearrange("p (ct t) -> p ct t",
                                                   ct=KC)[:, ct],
                                     dbF.rearrange("p (n t) -> p t n",
                                                   t=c.TB),
                                     axis=AX.X), inc=True)
                g_mul[g] = v_red[g]
            P.run("vector", lambda e: e.drain())
            P.dwait("vector", "syg")
            P.wait("vector", "ACT", a_hc[cb_ * KC + KC - 1])
            P.run("vector", lambda e: e.tensor_add(ytF, ytF, xdS))
            P.run("vector", lambda e: e.drain())
            vd = P.run("vector", lambda e: e.tensor_mul(ygF, ytF, szS),
                       drain=True)
            prev_vd = vd
            P.wait("gpsimd", "DVE", vd)
            P.dma("gpsimd", "syg",
                  yg_d[dirn].ap().rearrange("(ct p) t -> p ct t", p=128)
                      [:, :, tt0:tt0 + c.TB],
                  ygF.rearrange("p (ct t) -> p ct t", ct=KC))
            if cb_ + 1 < NCB:
                P.dwait("scalar", "syg")

    def mamba_out(b):
        P.dwait("sync", "syg")
        ow = abf16(MC_OW, u_ow)
        owv = ow.rearrange("p (d k md) -> p d k md", d=2, k=KC)
        ygm = abf16(MC_YG, u_ygm)
        ygmv = ygm.rearrange("p (d ct t) -> p d ct t", d=2, ct=KC)
        for dirn in range(2):
            for k in range(KC):
                P.dma("sync", "owl", owv[:, dirn, k],
                      boutw[dirn].ap()[k * 128:(k + 1) * 128, :])
        o_ev = [None, None]
        x2_st = [None, None]
        x1_rd = [None, None]
        ym_last = None
        for m in range(LT):
            P.wait("sync", "PE", ym_last)
            for dirn in range(2):
                P.dma("sync", "ygl", ygmv[:, dirn],
                      yg_d[dirn].ap().rearrange("(ct p) t -> p ct t", p=128)
                      [:, :, m * 128:(m + 1) * 128])
            x1sl = af32(MC_X1 + (m % 2) * KD * 128, KD * 128)
            x1s = "x1a" if m % 2 == 0 else "x1b"
            P.wait("sync", "DVE", x1_rd[m % 2])
            P.dma("sync", x1s, x1sl,
                  x1_d.ap()[b * c.L + m * 128:b * c.L + (m + 1) * 128, :])
            P.dwait("tensor", "ygl")
            if m == 0:
                P.dwait("tensor", "owl")
            P.dwait("vector", "sx2")
            x2sl = af32(MC_X2 + (m % 2) * KD * 128, KD * 128)
            for f in range(NF):
                so = f % 2
                po = pb[so]
                P.wait("tensor", "DVE", o_ev[so])

                def mmo(e, f=f, po=po):
                    first = True
                    ins = None
                    for dirn in range(2):
                        for k in range(KC):
                            ins = e.matmul(
                                po[:, 0:FD],
                                ygmv[:, dirn, k],
                                owv[:, dirn, k, f * FD:(f + 1) * FD],
                                start=first,
                                stop=(dirn == 1 and k == KC - 1))
                            first = False
                    return ins
                vp = P.run("tensor", mmo, inc=True)
                ym_last = vp
                P.wait("vector", "PE", vp)
                P.dwait("vector", x1s)
                vd = P.run("vector", lambda e, f=f, po=po, x1sl=x1sl,
                           x2sl=x2sl: e.tensor_add(
                               x2sl[:, f * FD:(f + 1) * FD],
                               x1sl[:, f * FD:(f + 1) * FD], po[:, 0:FD]),
                           inc=True)
                o_ev[so] = vd
            vfin = P.drain_inc("vector")
            x1_rd[m % 2] = vfin
            P.wait("gpsimd", "DVE", vfin)
            P.dma(
                "gpsimd", "sx2",
                x2_d.ap()[b * c.L + m * 128:b * c.L + (m + 1) * 128, :],
                x2sl)

    prep()
    P.barrier()
    ffn_call(0)
    P.barrier()
    for b in range(c.BL):
        mamba_batch(b)
    P.barrier()
    ffn_call(1)
    for s in ("sfo", "sx2", "syg"):
        P.dwait("gpsimd", s)

    P.emit_all()
    ctx.close()
    return nc


def prep_inputs(cfg, inputs, ncores=NCORES):
    c = cfg
    KC = c.CI // 128
    g = lambda k: np.asarray(inputs[k])
    x = g("L0_attn_x").reshape(ncores, c.BL * c.L, c.D)
    T = lambda a: np.ascontiguousarray(np.swapaxes(a, -1, -2))
    common = {
        "fwgT": T(g("L0_ffn_wg")),
        "fwuT": T(g("L0_ffn_wu")),
        "fwdT": T(g("L0_ffn_wd")),
        "fgT": T(g("L0_ffn_gate_w")),
        "ident": np.eye(128, dtype=ml_dtypes.bfloat16),
        "antid": np.ascontiguousarray(
            np.flipud(np.eye(128)).astype(ml_dtypes.bfloat16)),
    }
    for sfx in ("1", "2"):
        common[f"inwT{sfx}"] = T(g(f"L0_attn_in_w{sfx}"))
        common[f"outwT{sfx}"] = T(g(f"L0_attn_out_w{sfx}"))
        common[f"xpwT{sfx}"] = T(g(f"L0_attn_xp_w{sfx}"))
        common[f"dtpwT{sfx}"] = T(g(f"L0_attn_dtp_w{sfx}"))
        common[f"convw{sfx}"] = np.ascontiguousarray(
            g(f"L0_attn_conv_w{sfx}").reshape(c.CI, 4)
            .reshape(KC, 128, 4).transpose(1, 0, 2).reshape(128, KC * 4))
        common[f"convb{sfx}"] = np.ascontiguousarray(
            g(f"L0_attn_conv_b{sfx}").reshape(KC, 128).T)
        common[f"dtpb{sfx}"] = np.ascontiguousarray(
            g(f"L0_attn_dtp_b{sfx}").reshape(KC, 128).T)
        common[f"dvec{sfx}"] = np.ascontiguousarray(
            g(f"L0_attn_Dvec{sfx}").reshape(KC, 128).T)
        common[f"alog{sfx}"] = np.ascontiguousarray(
            g(f"L0_attn_A_log{sfx}").reshape(KC, 128, c.NS)
            .transpose(1, 0, 2).reshape(128, KC * c.NS))
    in_maps = []
    for i in range(ncores):
        m = dict(common)
        m["x"] = np.ascontiguousarray(x[i])
        in_maps.append(m)
    return in_maps


def _install_fast_compile():
    try:
        import concourse.bass_utils as bu
        if getattr(bu, "_fast_compile_installed", False):
            return
        from pathlib import Path as _Path

        def bir_verify_and_optimise(tmpdir, inp="bir.json", outp="file.neff",
                                    arch=None, *, dve_root=None):
            cmd = [
                bu.get_walrus_driver(),
                "--pass",
                ",".join([
                    "runtime_memory_reservation",
                    "lower_act",
                    "lower_dve",
                    "lower_ap_offset",
                    "codegen",
                    "neff_packager",
                ]),
                "-i", inp,
                "--neff-output-filename", outp,
                "--enable-birsim=true",
                "--mem-mode=physical",
                "--policy=0",
                "--enable-ldw-opt=false",
                "--assign-static-dmas-to-sp=false",
                "--dram-page-size=256",
                "--enable-neff-debug-info=true",
                "--jobs", "8",
                *bu.get_walrus_args(
                    bu.get_bir_arch(tmpdir, inp) if arch is None else arch,
                    tmpdir, dve_root=dve_root),
            ]
            result = bu.run_command(cmd, cwd=tmpdir)
            if result is not None:
                (_Path(tmpdir) / "log.txt").write_text(result.stdout)
            return str(_Path(tmpdir) / outp)

        bu.bir_verify_and_optimise = bir_verify_and_optimise
        bu._fast_compile_installed = True
    except Exception:
        pass


def kernel(**inputs):
    _install_fast_compile()
    cfg = FULL
    in_maps = prep_inputs(cfg, inputs)
    nc = build_program(cfg)
    res = run_bass_kernel_spmd(nc, in_maps, core_ids=list(range(NCORES)))
    out = np.concatenate([r["out"] for r in res.results], axis=0)
    return out.reshape(32, cfg.L, cfg.D)


# version 10 (final)
# speedup vs baseline: 1.1290x; 1.0691x over previous
import numpy as np
import ml_dtypes
import concourse.bass as bass
import concourse.mybir as mybir
from concourse.bass_utils import run_bass_kernel_spmd
from contextlib import ExitStack

F32 = mybir.dt.float32
BF16 = mybir.dt.bfloat16
AF = mybir.ActivationFunctionType
ALU = mybir.AluOpType
AX = mybir.AxisListType

NCORES = 8


class Cfg:
    def __init__(self, D=1024, CI=2048, NS=16, RDT=64, L=2048, BL=4, E=8,
                 HID=4096, FF_G=1024, TA=512, TB=256, ftscap=512, fdcap=512):
        self.ftscap, self.fdcap = ftscap, fdcap
        self.D, self.CI, self.NS, self.RDT = D, CI, NS, RDT
        self.L, self.BL, self.E, self.HID = L, BL, E, HID
        self.FF_G, self.TA, self.TB = FF_G, TA, TB
        self.C2 = 2 * CI
        self.T = BL * L
        assert self.T % FF_G == 0 and FF_G % 128 == 0
        assert L % TA == 0 and L % TB == 0 and TA % 128 == 0 and TB % 128 == 0
        assert D % 128 == 0 and CI % 128 == 0 and HID % 128 == 0
        assert HID % 4 == 0 and (HID // 4) % 128 == 0
        assert RDT <= 128 and NS <= 16 and TA <= 512 and TB <= 512


FULL = Cfg()
ENGINES = ("sync", "scalar", "vector", "tensor", "gpsimd")


class Counter:
    __slots__ = ("h", "n")

    def __init__(self, h):
        self.h = h
        self.n = 0


class Prog:
    ENG2SEM = {"tensor": "PE", "scalar": "ACT", "vector": "DVE",
               "gpsimd": "PL", "sync": "SY"}

    def __init__(self, nc):
        self.nc = nc
        self.items = []

    def init_sems(self, ctx):
        e = ctx.enter_context
        self._ctx = ctx
        self.sems = {}
        self.streams = {}
        for name in ("PE", "ACT", "DVE", "PL", "SY"):
            self.sems[name] = Counter(e(self.nc.semaphore(name)))

    def on(self, eng, fn):
        self.items.append((eng, fn))

    def stream(self, name):
        if name not in self.streams:
            self.streams[name] = Counter(
                self._ctx.enter_context(self.nc.semaphore("sm_" + name)))
        return self.streams[name]

    def dma(self, queue, stream, out, in_, **kw):
        sem = self.stream(stream)
        sem.n += 16
        v = sem.n

        def fn(e, out=out, in_=in_, kw=kw, sem=sem):
            e.dma_start(out=out, in_=in_, **kw).then_inc(sem.h, 16)

        self.on(queue, fn)
        return v

    def dwait(self, eng, stream, val=None):
        if stream not in self.streams:
            return
        sem = self.streams[stream]
        v = sem.n if val is None else val
        if v == 0:
            return

        def fn(e, sem=sem, v=v):
            e.wait_ge(sem.h, v)

        self.on(eng, fn)

    def wait(self, eng, sem_name, val):
        if val is None:
            return
        sem = self.sems[sem_name]

        def fn(e, sem=sem, val=val):
            e.wait_ge(sem.h, val)

        self.on(eng, fn)

    def run(self, eng, fn, inc=False, drain=False):
        sem = self.sems[self.ENG2SEM[eng]]
        v = None
        if inc or drain:
            sem.n += 1
            v = sem.n
        if inc:
            def wrapped(e, fn=fn, sem=sem):
                fn(e).then_inc(sem.h, 1)
            self.on(eng, wrapped)
        elif drain:
            def wrapped(e, fn=fn, sem=sem):
                fn(e)
                e.drain().then_inc(sem.h, 1)
            self.on(eng, wrapped)
        else:
            self.on(eng, fn)
        return v

    def drain_inc(self, eng):
        sem = self.sems[self.ENG2SEM[eng]]
        sem.n += 1
        v = sem.n

        def fn(e, sem=sem):
            e.drain().then_inc(sem.h, 1)

        self.on(eng, fn)
        return v

    def barrier(self, engines=ENGINES):
        for s in list(self.streams):
            self.dwait("sync", s)
        vals = {}
        for eng in engines:
            sem = self.sems[self.ENG2SEM[eng]]
            nprev = sem.n
            sem.n += 1
            vals[eng] = sem.n
            if eng in ("vector", "scalar", "gpsimd"):
                def fn(e, sem=sem):
                    e.drain().then_inc(sem.h, 1)
            else:
                def fn(e, sem=sem, nprev=nprev):
                    if nprev:
                        e.wait_ge(sem.h, nprev)
                    e.nop().then_inc(sem.h, 1)
            self.on(eng, fn)
        for eng in engines:
            for other in engines:
                if other != eng:
                    self.wait(eng, self.ENG2SEM[other], vals[other])

    def emit_all(self):
        nc = self.nc
        with nc.Block() as block:
            for eng_name in ENGINES:
                def body(e, eng_name=eng_name):
                    for eng, fn in self.items:
                        if eng == eng_name:
                            fn(e)
                getattr(block, eng_name)(body)


def build_program(cfg: Cfg, debug_outs=False):
    nc = bass.Bass("TRN2", target_bir_lowering=False)
    c = cfg
    KD = c.D // 128
    KC = c.CI // 128
    KH = c.HID // 128
    TPG = c.FF_G // 128
    NG = c.T // c.FF_G
    NCA = c.L // c.TA
    NCB = c.L // c.TB
    LT = c.L // 128
    FTS = min(c.ftscap, c.FF_G)
    NTS = c.FF_G // FTS
    FD = min(c.fdcap, c.D)
    NF = c.D // FD
    HQ = c.HID // 4
    NHT = HQ // 128
    NXP = c.RDT + 2 * c.NS

    def din(name, shape, dt=F32):
        return nc.dram_tensor(name, list(shape), dt, kind="ExternalInput")

    x_d = din("x", (c.T, c.D))
    fwgT = din("fwgT", (c.E, c.D, c.HID))
    fwuT = din("fwuT", (c.E, c.D, c.HID))
    fwdT = din("fwdT", (c.E, c.HID, c.D))
    fgT = din("fgT", (c.D, c.E))
    inwT = [din("inwT1", (c.D, c.C2)), din("inwT2", (c.D, c.C2))]
    outwT = [din("outwT1", (c.CI, c.D)), din("outwT2", (c.CI, c.D))]
    xpwT = [din("xpwT1", (c.CI, NXP)), din("xpwT2", (c.CI, NXP))]
    dtpwT = [din("dtpwT1", (c.RDT, c.CI)), din("dtpwT2", (c.RDT, c.CI))]
    convw = [din("convw1", (128, KC * 4)), din("convw2", (128, KC * 4))]
    convb = [din("convb1", (128, KC)), din("convb2", (128, KC))]
    dtpb = [din("dtpb1", (128, KC)), din("dtpb2", (128, KC))]
    dvec = [din("dvec1", (128, KC)), din("dvec2", (128, KC))]
    alog = [din("alog1", (128, KC * c.NS)), din("alog2", (128, KC * c.NS))]
    ident_d = din("ident", (128, 128), BF16)
    antid_d = din("antid", (128, 128), BF16)
    out_d = nc.dram_tensor("out", [c.T, c.D], F32, kind="ExternalOutput")

    def scr(name, shape, dt):
        return nc.dram_tensor(name, list(shape), dt, kind="Internal")

    bwg = scr("bwg", (c.E, c.D, c.HID), BF16)
    bwu = scr("bwu", (c.E, c.D, c.HID), BF16)
    bwd = scr("bwd", (c.E, c.HID, c.D), BF16)
    bgT = scr("bgT", (c.D, c.E), BF16)
    binw = [scr("binw1", (c.D, c.C2), BF16), scr("binw2", (c.D, c.C2), BF16)]
    boutw = [scr("boutw1", (c.CI, c.D), BF16), scr("boutw2", (c.CI, c.D), BF16)]
    bxpw = [scr("bxpw1", (c.CI, NXP), BF16), scr("bxpw2", (c.CI, NXP), BF16)]
    bdtpw = [scr("bdtpw1", (c.RDT, c.CI), BF16),
             scr("bdtpw2", (c.RDT, c.CI), BF16)]
    dkind = "ExternalOutput" if debug_outs else "Internal"
    x1_d = nc.dram_tensor("x1", [c.T, c.D], F32, kind=dkind)
    x2_d = nc.dram_tensor("x2", [c.T, c.D], F32, kind=dkind)
    xnTm_d = scr("xnTm", (c.D, c.L), BF16)
    xnTr_d = scr("xnTr", (c.D, c.L), BF16)
    sDel = scr("sDel", (c.CI, c.L), F32)
    sDu = scr("sDu", (c.CI, c.L), F32)
    sSz = scr("sSz", (c.CI, c.L), F32)
    sXd = scr("sXd", (c.CI, c.L), F32)
    sB = scr("sB", (c.NS, c.L), F32)
    sC = scr("sC", (c.NS, c.L), F32)
    yg_d = [scr("yg1", (c.CI, c.L), BF16), scr("yg2", (c.CI, c.L), BF16)]

    ctx = ExitStack()
    e_ = ctx.enter_context
    P = Prog(nc)
    P.init_sems(ctx)

    ident = e_(nc.sbuf_tensor("identsb", [128, 128], BF16))
    antid = e_(nc.sbuf_tensor("antidsb", [128, 128], BF16))
    ggT = e_(nc.sbuf_tensor("ggTsb", [128, KD * c.E], BF16))
    A_sb = e_(nc.sbuf_tensor("A_sb", [128, 2 * KC * c.NS], F32))
    cw_sb = e_(nc.sbuf_tensor("cw_sb", [128, 2 * KC * 4], F32))
    cb_sb = e_(nc.sbuf_tensor("cb_sb", [128, 2 * KC], F32))
    db_sb = e_(nc.sbuf_tensor("db_sb", [128, 2 * KC], F32))
    dv_sb = e_(nc.sbuf_tensor("dv_sb", [128, 2 * KC], F32))
    ones1 = e_(nc.sbuf_tensor("ones1", [1, 128], F32))
    stats = e_(nc.sbuf_tensor("stats", [128, 128], F32))
    glog = e_(nc.sbuf_tensor("glog", [128, TPG * c.E], F32))
    gtmp = e_(nc.sbuf_tensor("gtmp", [128, 2 * TPG * c.E], F32))
    gsc = e_(nc.sbuf_tensor("gsc", [128, 8 * TPG], F32))
    comb = e_(nc.sbuf_tensor("comb", [128, TPG * c.E], F32))

    AR_U = nc.sbuf_bytes_remaining // 4 - 512
    arena = e_(nc.sbuf_tensor("arena", [128, AR_U], F32))

    def af32(off, n):
        assert off + n <= AR_U, (off, n, AR_U)
        return arena[:, off:off + n]

    def abf16(off, nf32):
        assert off + nf32 <= AR_U, (off, nf32, AR_U)
        return arena[:, off:off + nf32].bitcast(BF16)

    pb = [e_(nc.psum_tensor(f"pb{i}", [128, 512], F32)) for i in range(6)]
    pT = e_(nc.psum_tensor("pT", [128, KD * 128], BF16))
    pGt = e_(nc.psum_tensor("pGt", [128, 512], F32))

    prep_ready = {}

    def prep():
        P.dma("sync", "cst", ident[:], ident_d.ap())
        P.dma("sync", "cst", antid[:], antid_d.ap())
        for d in range(2):
            P.dma("sync", "cst", cw_sb[:, d * KC * 4:(d + 1) * KC * 4], convw[d].ap())
            P.dma("sync", "cst", cb_sb[:, d * KC:(d + 1) * KC], convb[d].ap())
            P.dma("sync", "cst", db_sb[:, d * KC:(d + 1) * KC], dtpb[d].ap())
            P.dma("sync", "cst", dv_sb[:, d * KC:(d + 1) * KC], dvec[d].ap())
        v_alog = None
        for d in range(2):
            v_alog = P.dma("sync", "cst",
                           A_sb[:, d * KC * c.NS:(d + 1) * KC * c.NS],
                           alog[d].ap())
        P.run("gpsimd", lambda e: e.memset(ones1[:], 1.0))
        order = []
        for ee in range(c.E):
            order += [(bwg, fwgT, ("wg", ee), ee), (bwu, fwuT, ("wu", ee), ee),
                      (bwd, fwdT, ("wd", ee), ee)]
        order.append((bgT, fgT, ("g", 0), None))
        for d in range(2):
            order += [(binw[d], inwT[d], ("inw", d), None),
                      (boutw[d], outwT[d], ("outw", d), None),
                      (bxpw[d], xpwT[d], ("xpw", d), None),
                      (bdtpw[d], dtpwT[d], ("dtpw", d), None)]
        for dst, src, key, ee in order:
            if key == ("g", 0):
                stream = "cg"
            elif ee is None:
                stream = "cm"
            else:
                stream = f"ce{ee}"
            if ee is None:
                P.dma("gpsimd", stream, dst.ap(), src.ap())
            else:
                P.dma("gpsimd", stream, dst.ap()[ee], src.ap()[ee])
        P.dwait("sync", "cg")
        P.dma("sync", "cst", ggT[:].rearrange("p (k e) -> p k e", e=c.E),
              bgT.ap().rearrange("(k p) e -> p k e", p=128))
        P.dwait("scalar", "cst")
        va = P.run("scalar", lambda e: e.activation(
            A_sb[:], A_sb[:], AF.Exp), drain=True)
        P.wait("vector", "ACT", va)
        P.run("vector", lambda e: e.tensor_scalar_mul(
            A_sb[:], A_sb[:], -1.0), drain=True)

    def ln_tiles(src_d, t0, ntiles, xs_off, consume):
        nD = KD * 128
        s1, s2 = stats[:, 0:16], stats[:, 16:32]
        mm_, vv = stats[:, 32:48], stats[:, 48:64]
        rstd, nmr = stats[:, 64:80], stats[:, 80:96]
        sq = af32(xs_off + 3 * nD, nD)
        rd_v = [None, None]
        sq_v = [None, None]
        va = None
        for j in range(ntiles):
            sl = af32(xs_off + (j % 2) * nD, nD)
            xstream = "xa" if j % 2 == 0 else "xb"
            P.wait("sync", "DVE", rd_v[j % 2])
            P.wait("sync", "ACT", sq_v[j % 2])
            P.dma("sync", xstream, sl,
                  src_d.ap()[t0 + j * 128:t0 + (j + 1) * 128, :])
            P.dwait("vector", xstream)
            rd_v[j % 2] = P.run("vector", lambda e, sl=sl, j=j: e.reduce_sum(
                s1[:, j:j + 1], sl, axis=AX.X), inc=True)
            P.dwait("scalar", xstream)
            P.wait("scalar", "ACT", sq_v[(j - 1) % 2] if j >= 1 else None)
            sq_v[j % 2] = P.run("scalar", lambda e, sl=sl, j=j: e.activation(
                sq, sl, AF.Square, accum_out=s2[:, j:j + 1]), inc=True)
        va = P.drain_inc("scalar")
        vd0 = P.drain_inc("vector")
        P.wait("vector", "ACT", va)
        P.run("vector", lambda e: e.tensor_scalar_mul(
            mm_[:, 0:ntiles], s1[:, 0:ntiles], 1.0 / nD))
        P.run("vector", lambda e: e.tensor_scalar_mul(
            vv[:, 0:ntiles], s2[:, 0:ntiles], 1.0 / nD))
        P.run("vector", lambda e: e.drain())
        P.run("vector", lambda e: e.tensor_tensor(
            s1[:, 0:ntiles], mm_[:, 0:ntiles], mm_[:, 0:ntiles], ALU.mult))
        P.run("vector", lambda e: e.drain())
        P.run("vector", lambda e: e.tensor_sub(
            vv[:, 0:ntiles], vv[:, 0:ntiles], s1[:, 0:ntiles]))
        P.run("vector", lambda e: e.drain())
        vd = P.run("vector", lambda e: e.tensor_scalar_add(
            vv[:, 0:ntiles], vv[:, 0:ntiles], 1e-5), drain=True)
        P.wait("scalar", "DVE", vd)
        va = P.run("scalar", lambda e: e.activation(
            rstd[:, 0:ntiles], vv[:, 0:ntiles], AF.Ln), drain=True)
        P.wait("vector", "ACT", va)
        P.run("vector", lambda e: e.tensor_scalar_mul(
            rstd[:, 0:ntiles], rstd[:, 0:ntiles], -0.5))
        vd = P.drain_inc("vector")
        P.wait("scalar", "DVE", vd)
        va = P.run("scalar", lambda e: e.activation(
            rstd[:, 0:ntiles], rstd[:, 0:ntiles], AF.Exp), drain=True)
        P.wait("vector", "ACT", va)
        P.run("vector", lambda e: e.tensor_mul(
            nmr[:, 0:ntiles], mm_[:, 0:ntiles], rstd[:, 0:ntiles]))
        P.run("vector", lambda e: e.drain())
        vd = P.run("vector", lambda e: e.tensor_scalar_mul(
            nmr[:, 0:ntiles], nmr[:, 0:ntiles], -1.0), drain=True)
        P.wait("scalar", "DVE", vd)
        xn_v = [None, None]
        xread_v = [vd, vd]
        P.wait("sync", "ACT", va)
        P.wait("sync", "DVE", vd)
        xread_v = [None, None]
        for j in range(ntiles):
            sl = af32(xs_off + (j % 2) * nD, nD)
            xstream = "xa" if j % 2 == 0 else "xb"
            xnsl = abf16(xs_off + 2 * nD + (j % 2) * (nD // 2), nD // 2)
            P.wait("sync", "ACT", xread_v[j % 2])
            P.wait("sync", "PE", xn_v[j % 2])
            P.dma("sync", xstream, sl,
                  src_d.ap()[t0 + j * 128:t0 + (j + 1) * 128, :])
            P.dwait("scalar", xstream)
            xread_v[j % 2] = P.run(
                "scalar", lambda e, sl=sl, xnsl=xnsl, j=j: e.activation(
                    xnsl, sl, AF.Identity, bias=nmr[:, j:j + 1],
                    scale=rstd[:, j:j + 1]), drain=True)
            P.wait("tensor", "ACT", xread_v[j % 2])
            xn_v[j % 2] = consume(j, xnsl)

    def emit_gate():
        n = TPG * c.E
        lv = glog[:, 0:n].rearrange("p (j e) -> p j e", e=c.E)
        m1 = gsc[:, 0:TPG]
        m2 = gsc[:, TPG:2 * TPG]
        ed = gsc[:, 2 * TPG:3 * TPG]
        p1h = gsc[:, 3 * TPG:4 * TPG]
        p2h = gsc[:, 4 * TPG:5 * TPG]
        t0v = gtmp[:, 0:n].rearrange("p (j e) -> p j e", e=c.E)
        t1v = gtmp[:, n:2 * n].rearrange("p (j e) -> p j e", e=c.E)
        b_ = lambda a: a[:, :, None].broadcast_to([128, TPG, c.E])
        P.run("vector", lambda e: e.reduce_max(m1[:, 0:TPG], lv, axis=AX.X))
        P.run("vector", lambda e: e.drain())
        P.run("vector", lambda e: e.tensor_tensor(t0v, lv, b_(m1), ALU.is_ge))
        P.run("vector", lambda e: e.drain())
        P.run("vector", lambda e: e.scalar_tensor_tensor(
            t1v, t0v, -1e30, lv, ALU.mult, ALU.add))
        P.run("vector", lambda e: e.drain())
        P.run("vector", lambda e: e.reduce_max(m2[:, 0:TPG], t1v, axis=AX.X))
        P.run("vector", lambda e: e.drain())
        P.run("vector", lambda e: e.tensor_tensor(t1v, t1v, b_(m2), ALU.is_ge))
        P.run("vector", lambda e: e.drain())
        vd = P.run("vector", lambda e: e.tensor_sub(
            m2[:, 0:TPG], m2[:, 0:TPG], m1[:, 0:TPG]), drain=True)
        P.wait("scalar", "DVE", vd)
        va = P.run("scalar", lambda e: e.activation(
            ed[:, 0:TPG], m2[:, 0:TPG], AF.Exp), drain=True)
        P.wait("vector", "ACT", va)
        P.run("vector", lambda e: e.tensor_scalar_add(
            p1h[:, 0:TPG], ed[:, 0:TPG], 1.0))
        P.run("vector", lambda e: e.drain())
        P.run("vector", lambda e: e.reciprocal(p1h[:, 0:TPG], p1h[:, 0:TPG]))
        P.run("vector", lambda e: e.drain())
        P.run("vector", lambda e: e.tensor_mul(
            p2h[:, 0:TPG], ed[:, 0:TPG], p1h[:, 0:TPG]))
        P.run("vector", lambda e: e.drain())
        P.run("vector", lambda e: e.tensor_scalar_mul(
            p1h[:, 0:TPG], p1h[:, 0:TPG], 0.5))
        P.run("vector", lambda e: e.tensor_scalar_mul(
            p2h[:, 0:TPG], p2h[:, 0:TPG], 0.5))
        P.run("vector", lambda e: e.drain())
        cv = comb[:, 0:n].rearrange("p (j e) -> p j e", e=c.E)
        P.run("vector", lambda e: e.tensor_mul(t0v, t0v, b_(p1h)))
        P.run("vector", lambda e: e.tensor_mul(t1v, t1v, b_(p2h)))
        P.run("vector", lambda e: e.drain())
        return P.run("vector", lambda e: e.tensor_add(cv, t0v, t1v), drain=True)

    u_xnt = KD * c.FF_G // 2
    u_acc = TPG * c.D
    u_hid = KH * c.FF_G // 2
    u_qp = 2 * (KD * HQ) // 2
    u_w = max(2 * u_qp, KH * c.D // 2)
    u_xs = 4 * KD * 128
    FF_XNT = 0
    FF_ACC = FF_XNT + u_xnt
    FF_HID = FF_ACC + u_acc
    FF_W = FF_HID + u_hid
    FF_XS = FF_W + u_w
    FF_SIL = FF_XS + u_xs
    u_sil = 2 * FTS
    assert FF_SIL + u_sil <= AR_U

    state = {"g_ev": [None, None], "u_ev": [None, None], "o_ev": [None, None],
             "m3_last": None, "hid_last": None, "gate_act": None}

    def ffn_call(call):
        src_d = x_d if call == 0 else x2_d
        dst_d = x1_d if call == 0 else out_d
        xnT = abf16(FF_XNT, u_xnt)
        hid = abf16(FF_HID, u_hid)
        acc = af32(FF_ACC, u_acc)
        wq = [abf16(FF_W + s * u_qp, u_qp) for s in range(2)]
        wd_all = abf16(FF_W, KH * c.D // 2)
        xnTv = xnT.rearrange("p (k t) -> p k t", t=c.FF_G)
        hidv = hid.rearrange("p (kh t) -> p kh t", t=c.FF_G)
        wdv = wd_all.rearrange("p (k d) -> p k d", k=KH)
        accv = acc.rearrange("p (m d) -> p m d", m=TPG)
        for g in range(NG):
            t0 = g * c.FF_G
            last_xnt = [None]

            def consume(j, xnsl):
                def tfn(e, xnsl=xnsl):
                    for k in range(KD):
                        ins = e.transpose(
                            pT[:, k * 128:(k + 1) * 128],
                            xnsl.rearrange("p (k t) -> p k t", k=KD)[:, k],
                            ident[:])
                    return ins
                vp = P.run("tensor", tfn, inc=True)
                P.wait("vector", "PE", vp)
                vd = P.run("vector", lambda e, j=j: e.tensor_copy(
                    xnTv[:, :, j * 128:(j + 1) * 128],
                    pT[:, 0:KD * 128].rearrange("p (k t) -> p k t", k=KD)),
                    drain=True)
                last_xnt[0] = vd
                P.wait("tensor", "DVE", vd)

                def gmm(e, j=j):
                    for k in range(KD):
                        ins = e.matmul(
                            pGt[:, 0:c.E],
                            xnTv[:, k, j * 128:(j + 1) * 128],
                            ggT[:].rearrange("p (k e) -> p k e", e=c.E)[:, k],
                            start=(k == 0), stop=(k == KD - 1))
                    return ins
                vp2 = P.run("tensor", gmm, inc=True)
                P.wait("scalar", "PE", vp2)
                va = P.run("scalar", lambda e, j=j: e.copy(
                    glog[:, j * c.E:(j + 1) * c.E], pGt[:, 0:c.E]), drain=True)
                state["gate_act"] = va
                P.wait("tensor", "ACT", va)
                return vp2

            ln_tiles(src_d, t0, TPG, FF_XS, consume)
            P.wait("vector", "ACT", state["gate_act"])
            emit_gate()
            for ee in range(c.E):
                P.wait("sync", "PE", state["m3_last"])
                P.dwait("sync", f"ce{ee}")
                qv = [None] * 4
                pe_q = [None] * 4

                def load_quarter(q):
                    sl = wq[q % 2]
                    ws = "wqa" if q % 2 == 0 else "wqb"
                    slv = sl.rearrange("p (m k h) -> p m k h", m=2, k=KD)
                    for k in range(KD):
                        P.dma("sync", ws, slv[:, 0, k],
                              bwg.ap()[ee, k * 128:(k + 1) * 128,
                                       q * HQ:(q + 1) * HQ])
                    for k in range(KD):
                        P.dma("sync", ws, slv[:, 1, k],
                              bwu.ap()[ee, k * 128:(k + 1) * 128,
                                       q * HQ:(q + 1) * HQ])

                load_quarter(0)
                load_quarter(1)
                P.wait("vector", "PE", state["m3_last"])
                ht_glob = 0
                for q in range(4):
                    P.dwait("tensor", "wqa" if q % 2 == 0 else "wqb")
                    if q == 0:
                        P.wait("tensor", "DVE", last_xnt[0])
                    slv = wq[q % 2].rearrange("p (m k h) -> p m k h",
                                              m=2, k=KD)
                    for ht in range(NHT):
                        kidx = q * NHT + ht
                        for ts_ in range(NTS):
                            sg = ht_glob % 2
                            pg, pu = pb[sg], pb[2 + sg]
                            P.wait("tensor", "ACT", state["g_ev"][sg])
                            P.wait("tensor", "DVE", state["u_ev"][sg])

                            def m1(e, slv=slv, ht=ht, ts_=ts_, pg=pg):
                                for k in range(KD):
                                    ins = e.matmul(
                                        pg[:, 0:FTS],
                                        slv[:, 0, k, ht * 128:(ht + 1) * 128],
                                        xnTv[:, k, ts_ * FTS:(ts_ + 1) * FTS],
                                        start=(k == 0), stop=(k == KD - 1))
                                return ins
                            v1 = P.run("tensor", m1, inc=True)

                            def m2(e, slv=slv, ht=ht, ts_=ts_, pu=pu):
                                for k in range(KD):
                                    ins = e.matmul(
                                        pu[:, 0:FTS],
                                        slv[:, 1, k, ht * 128:(ht + 1) * 128],
                                        xnTv[:, k, ts_ * FTS:(ts_ + 1) * FTS],
                                        start=(k == 0), stop=(k == KD - 1))
                                return ins
                            v2 = P.run("tensor", m2, inc=True)
                            pe_q[q] = v2
                            P.wait("scalar", "PE", v1)
                            sil = af32(FF_SIL + sg * FTS, FTS)
                            va = P.run("scalar",
                                       lambda e, pg=pg, sil=sil: e.activation(
                                           sil, pg[:, 0:FTS], AF.Silu),
                                       inc=True)
                            state["g_ev"][sg] = va
                            P.wait("vector", "ACT", va)
                            P.wait("vector", "PE", v2)
                            hcols = hidv[:, kidx, ts_ * FTS:(ts_ + 1) * FTS]
                            vd = P.run("vector",
                                       lambda e, sil=sil, pu=pu, hcols=hcols:
                                       e.tensor_tensor(hcols, sil,
                                                       pu[:, 0:FTS],
                                                       ALU.mult), inc=True)
                            state["u_ev"][sg] = vd
                            ht_glob += 1
                    if q + 2 <= 3:
                        P.wait("sync", "PE", pe_q[q])
                        load_quarter(q + 2)
                    elif q == 2:
                        P.wait("sync", "PE", pe_q[2])
                        for k in range(KH // 2):
                            P.dma("sync", "wdl", wdv[:, k],
                                  bwd.ap()[ee, k * 128:(k + 1) * 128, :])
                state["hid_last"] = P.drain_inc("vector")
                P.wait("sync", "PE", pe_q[3])
                for k in range(KH // 2, KH):
                    P.dma("sync", "wdl", wdv[:, k],
                          bwd.ap()[ee, k * 128:(k + 1) * 128, :])
                P.dwait("tensor", "wdl")
                P.wait("tensor", "DVE", state["hid_last"])
                for m in range(TPG):
                    for f in range(NF):
                        so = (m * NF + f) % 2
                        po = pb[4 + so]
                        P.wait("tensor", "DVE", state["o_ev"][so])

                        def m3(e, m=m, f=f, po=po):
                            for k in range(KH):
                                ins = e.matmul(
                                    po[:, 0:FD],
                                    hidv[:, k, m * 128:(m + 1) * 128],
                                    wdv[:, k, f * FD:(f + 1) * FD],
                                    start=(k == 0), stop=(k == KH - 1))
                            return ins
                        vp = P.run("tensor", m3, inc=True)
                        state["m3_last"] = vp
                        P.wait("vector", "PE", vp)
                        accsl = accv[:, m, f * FD:(f + 1) * FD]
                        ccol = comb[:, m * c.E + ee:m * c.E + ee + 1]
                        if ee == 0:
                            vd = P.run("vector",
                                       lambda e, accsl=accsl, po=po,
                                       ccol=ccol: e.tensor_scalar(
                                           accsl, po[:, 0:FD], ccol, None,
                                           ALU.mult), inc=True)
                        else:
                            vd = P.run("vector",
                                       lambda e, accsl=accsl, po=po,
                                       ccol=ccol: e.scalar_tensor_tensor(
                                           accsl, po[:, 0:FD], ccol, accsl,
                                           ALU.mult, ALU.add), inc=True)
                        state["o_ev"][so] = vd
            vacc = P.drain_inc("vector")
            fin_rd = [vacc, vacc]
            for j in range(TPG):
                sl = af32(FF_XS + (j % 2) * KD * 128, KD * 128)
                xstream = "xa" if j % 2 == 0 else "xb"
                P.wait("sync", "DVE", fin_rd[j % 2])
                P.dwait("sync", "sfo")
                P.dma("sync", xstream, sl,
                      src_d.ap()[t0 + j * 128:t0 + (j + 1) * 128, :])
                P.dwait("vector", xstream)
                vd = P.run("vector", lambda e, sl=sl, j=j: e.tensor_add(
                    sl, sl, accv[:, j]), drain=True)
                fin_rd[j % 2] = vd
                P.wait("gpsimd", "DVE", vd)
                P.dma(
                    "gpsimd", "sfo",
                    dst_d.ap()[t0 + j * 128:t0 + (j + 1) * 128, :], sl)
            P.barrier()

    u_xi = KC * (c.TA + 3)
    u_xc = KC * c.TA
    u_xcb = KC * c.TA // 2
    u_xnt_m = KD * c.TA // 2
    u_inw = KD * c.C2 // 2
    u_xpw = (KC * NXP + 1) // 2
    u_dtpw = c.CI // 2
    u_dtbf = c.TA // 2
    u_stg = 10 * c.TA
    MA_XI = 0
    MA_XC = MA_XI + u_xi
    MA_XCB = MA_XC + u_xc
    MA_XNT = MA_XCB + u_xcb
    MA_INW = MA_XNT + u_xnt_m
    MA_XPW = MA_INW + u_inw
    MA_DTPW = MA_XPW + u_xpw
    MA_DTBF = MA_DTPW + u_dtpw
    MA_STG = MA_DTBF + u_dtbf
    MA_BST = MA_STG + u_stg
    assert MA_BST + 2 * c.TA <= AR_U

    u_mb = KC * c.TB
    u_br = c.NS * c.TB
    MB_DEL = 0
    MB_DU = MB_DEL + u_mb
    MB_SZ = MB_DU + u_mb
    MB_XD = MB_SZ + u_mb
    MB_BR = MB_XD + u_mb
    MB_CR = MB_BR + u_br
    MB_S = MB_CR + u_br
    MB_DA = MB_S + 2 * u_br
    MB_DB = MB_DA + u_br
    MB_YTF = MB_DB + 2 * u_br
    MB_YGF = MB_DA
    MB_H = MB_YTF + u_mb
    assert MB_H + KC * c.NS <= AR_U

    u_ygm = 2 * (KC * 128 // 2)
    u_ow = 2 * (KC * c.D // 2)
    MC_YG = 0
    MC_OW = MC_YG + u_ygm
    MC_X1 = MC_OW + u_ow
    MC_X2 = MC_X1 + 2 * KD * 128
    assert MC_X2 + 2 * KD * 128 <= AR_U

    def mamba_batch(b):
        stg = abf16(MA_STG, KD * 128)
        stgv = stg.rearrange("p (r k t) -> p r k t", r=2, k=KD)
        st_v = [[None, None], [None, None]]

        def consume(j, xnsl):
            vp_last = None
            for rev in range(2):
                idm = antid if rev else ident

                def tfn(e, xnsl=xnsl, idm=idm):
                    for k in range(KD):
                        ins = e.transpose(
                            pT[:, k * 128:(k + 1) * 128],
                            xnsl.rearrange("p (k t) -> p k t", k=KD)[:, k],
                            idm[:])
                    return ins
                vp = P.run("tensor", tfn, inc=True)
                vp_last = vp
                P.wait("vector", "PE", vp)
                P.dwait("vector", "slnm" if rev == 0 else "slnr")
                dst = stgv[:, rev]
                vd = P.run("vector", lambda e, dst=dst: e.tensor_copy(
                    dst, pT[:, 0:KD * 128].rearrange("p (k t) -> p k t",
                                                     k=KD)), drain=True)
                P.wait("gpsimd", "DVE", vd)
                jj = j if rev == 0 else (LT - 1 - j)
                tgt = xnTm_d if rev == 0 else xnTr_d
                P.dma(
                    "gpsimd", "slnm" if rev == 0 else "slnr",
                    tgt.ap().rearrange("(k p) t -> p k t", p=128)
                       [:, :, jj * 128:(jj + 1) * 128],
                    dst)
                P.wait("tensor", "DVE", vd)
            return vp_last

        ln_tiles(x1_d, b * c.L, LT, MA_STG + KD * 128, consume)
        P.barrier()
        for dirn in range(2):
            mamba_front(b, dirn)
            P.barrier()
            mamba_scan(b, dirn)
            P.barrier()
        mamba_out(b)
        P.barrier()

    def mamba_front(b, dirn):
        xnsrc = xnTm_d if dirn == 0 else xnTr_d
        inw = abf16(MA_INW, u_inw)
        inwv = inw.rearrange("p (k m) -> p k m", k=KD)
        dtpw = abf16(MA_DTPW, u_dtpw)
        xi = af32(MA_XI, u_xi)
        xiv = xi.rearrange("p (ct t) -> p ct t", ct=KC)
        xc = af32(MA_XC, u_xc)
        xcv = xc.rearrange("p (ct t) -> p ct t", ct=KC)
        xcb = abf16(MA_XCB, u_xcb)
        xcbv = xcb.rearrange("p (k t) -> p k t", k=KC)
        dtbf = abf16(MA_DTBF, u_dtbf)
        P.dwait("sync", "cm")
        for k in range(KD):
            P.dma("sync", "mw", inwv[:, k],
                  binw[dirn].ap()[k * 128:(k + 1) * 128, :])
        xpw_full = abf16(MA_XPW, u_xpw)
        for k in range(KC):
            P.dma("sync", "mw",
                  xpw_full[:, k * NXP:(k + 1) * NXP],
                  bxpw[dirn].ap()[k * 128:(k + 1) * 128, :])
        P.dma("sync", "mw", dtpw[0:c.RDT, 0:c.CI], bdtpw[dirn].ap())
        vz = P.run("gpsimd", lambda e: e.memset(xiv[:, :, 0:3], 0.0),
                   drain=True)
        P.wait("scalar", "PL", vz)
        pe_ev = [None, None]
        dtp_ev = [None, None]
        sz_st = [None, None]
        xd_st = [None, None]
        dl_st = [None, None]
        du_st = [None, None]
        du_rd = [None, None]
        bc_st = None
        prev_vcv = None
        xz_last = None
        xp_last = None
        for ch in range(NCA):
            tt0 = ch * c.TA
            xnt = abf16(MA_XNT, u_xnt_m)
            xntv = xnt.rearrange("p (k t) -> p k t", k=KD)
            P.wait("sync", "PE", xz_last)
            for k in range(KD):
                P.dma("sync", "xnt", xntv[:, k],
                      xnsrc.ap()[k * 128:(k + 1) * 128, tt0:tt0 + c.TA])
            P.dwait("tensor", "xnt")
            if ch == 0:
                P.dwait("tensor", "mw")
            if ch > 0:
                P.wait("scalar", "DVE", prev_vcv)
                P.run("scalar", lambda e: e.copy(
                    xiv[:, :, 0:3], xiv[:, :, c.TA:c.TA + 3]))
                P.run("scalar", lambda e: e.drain())
            for m in range(2 * KC):
                sg = m % 2
                pg = pb[sg]
                g1 = max(pe_ev[sg] or 0, (dtp_ev[0] or 0) if sg == 0 else 0)
                P.wait("tensor", "ACT", g1 if g1 else None)

                def mmx(e, m=m, pg=pg):
                    for k in range(KD):
                        ins = e.matmul(
                            pg[:, 0:c.TA],
                            inwv[:, k, m * 128:(m + 1) * 128],
                            xntv[:, k],
                            start=(k == 0), stop=(k == KD - 1))
                    return ins
                vp = P.run("tensor", mmx, inc=True)
                xz_last = vp
                P.wait("scalar", "PE", vp)
                if m < KC:
                    dst = xiv[:, m, 3:3 + c.TA]
                    va = P.run("scalar", lambda e, dst=dst, pg=pg:
                               e.copy(dst, pg[:, 0:c.TA]), inc=True)
                else:
                    P.dwait("scalar", "ssz")
                    szst = af32(MA_STG + 6 * c.TA + sg * c.TA, c.TA)
                    va = P.run("scalar", lambda e, szst=szst, pg=pg:
                               e.activation(szst, pg[:, 0:c.TA], AF.Silu),
                               drain=True)
                    P.wait("gpsimd", "ACT", va)
                    P.dma(
                        "gpsimd", "ssz",
                        sSz.ap()[(m - KC) * 128:(m - KC + 1) * 128,
                                 tt0:tt0 + c.TA], szst)
                pe_ev[sg] = va
            vxi = P.drain_inc("scalar")
            P.wait("vector", "ACT", vxi)
            for ct in range(KC):
                xitile = xiv[:, ct]
                xct = xcv[:, ct]
                cwc = cw_sb[:, dirn * KC * 4 + ct * 4:dirn * KC * 4 + ct * 4 + 4]

                def conv(e, xitile=xitile, xct=xct, cwc=cwc):
                    e.tensor_scalar(xct, xitile[:, 0:c.TA], cwc[:, 0:1],
                                    None, ALU.mult)
                    ins = None
                    for jj in range(1, 4):
                        e.drain()
                        ins = e.scalar_tensor_tensor(
                            xct, xitile[:, jj:jj + c.TA], cwc[:, jj:jj + 1],
                            xct, ALU.mult, ALU.add)
                    return ins
                P.run("vector", conv)
            vcv = P.drain_inc("vector")
            prev_vcv = vcv
            P.wait("scalar", "DVE", vcv)
            for ct in range(KC):
                xct = xcv[:, ct]
                P.run("scalar", lambda e, xct=xct, ct=ct: e.activation(
                    xct, xct, AF.Silu,
                    bias=cb_sb[:, dirn * KC + ct:dirn * KC + ct + 1]))
                P.run("scalar", lambda e: e.drain())
                P.dwait("scalar", "sxd")
                xdst = af32(MA_STG + 8 * c.TA + (ct % 2) * c.TA, c.TA)
                va = P.run("scalar", lambda e, xct=xct, ct=ct, xdst=xdst:
                           e.activation(xdst, xct, AF.Identity,
                                        scale=dv_sb[:, dirn * KC + ct:
                                                    dirn * KC + ct + 1]),
                           drain=True)
                P.wait("gpsimd", "ACT", va)
                P.dma(
                    "gpsimd", "sxd", sXd.ap()[ct * 128:(ct + 1) * 128,
                                              tt0:tt0 + c.TA], xdst)
            vsl = P.drain_inc("scalar")
            P.wait("vector", "ACT", vsl)
            P.wait("vector", "PE", xp_last)
            vcb = P.run("vector", lambda e: e.tensor_copy(xcb, xc),
                        drain=True)
            P.wait("tensor", "DVE", vcb)
            parts = [(0, c.RDT, pb[2]), (c.RDT, c.NS, pb[3]),
                     (c.RDT + c.NS, c.NS, pb[5])]
            vp = None
            for (off, mlen, pp) in parts:
                def mmxp(e, off=off, mlen=mlen, pp=pp):
                    ins = None
                    for k in range(KC):
                        ins = e.matmul(
                            pp[0:mlen, 0:c.TA],
                            xpw_full[:, k * NXP + off:k * NXP + off + mlen],
                            xcbv[:, k],
                            start=(k == 0), stop=(k == KC - 1))
                    return ins
                vp = P.run("tensor", mmxp, inc=True)
            xp_last = vp
            P.wait("scalar", "PE", vp)
            P.dwait("scalar", "sbc")
            va = P.run("scalar", lambda e: e.copy(
                dtbf[0:c.RDT, 0:c.TA], pb[2][0:c.RDT, 0:c.TA]), drain=True)
            bst = af32(MA_BST, c.TA)
            cst = af32(MA_BST + c.TA, c.TA)
            P.run("scalar", lambda e, bst=bst: e.copy(
                bst[0:c.NS, :], pb[3][0:c.NS, 0:c.TA]))
            P.run("scalar", lambda e: e.drain())
            vc = P.run("scalar", lambda e, cst=cst: e.copy(
                cst[0:c.NS, :], pb[5][0:c.NS, 0:c.TA]), drain=True)
            P.wait("gpsimd", "ACT", vc)
            P.dma("gpsimd", "sbc", sB.ap()[:, tt0:tt0 + c.TA],
                  bst[0:c.NS, :])
            P.dma("gpsimd", "sbc", sC.ap()[:, tt0:tt0 + c.TA],
                  cst[0:c.NS, :])
            P.wait("tensor", "ACT", va)
            for m in range(KC):
                sg = m % 2
                pp = pb[sg * 4]
                P.wait("tensor", "ACT", dtp_ev[sg])

                def mmd(e, m=m, pp=pp):
                    return e.matmul(
                        pp[:, 0:c.TA],
                        dtpw[0:c.RDT, m * 128:(m + 1) * 128],
                        dtbf[0:c.RDT, 0:c.TA],
                        start=True, stop=True)
                vp = P.run("tensor", mmd, inc=True)
                P.wait("scalar", "PE", vp)
                est = af32(MA_STG + sg * c.TA, c.TA)
                dst_ = af32(MA_STG + 2 * c.TA + sg * c.TA, c.TA)
                dust = af32(MA_STG + 4 * c.TA + sg * c.TA, c.TA)
                P.dwait("scalar", "sdl")
                P.wait("scalar", "DVE", du_rd[sg])
                P.run("scalar", lambda e, est=est, pp=pp, m=m: e.activation(
                    est, pp[:, 0:c.TA], AF.Exp,
                    bias=db_sb[:, dirn * KC + m:dirn * KC + m + 1]))
                P.run("scalar", lambda e: e.drain())
                va = P.run("scalar", lambda e, est=est, dst_=dst_:
                           e.activation(dst_, est, AF.Ln, bias=1.0),
                           drain=True)
                dtp_ev[sg] = va
                P.wait("gpsimd", "ACT", va)
                P.dma(
                    "gpsimd", "sdl", sDel.ap()[m * 128:(m + 1) * 128,
                                               tt0:tt0 + c.TA], dst_)
                P.wait("vector", "ACT", va)
                P.dwait("vector", "sdu")
                xct = xcv[:, m]
                vd = P.run("vector", lambda e, dst_=dst_, dust=dust, xct=xct:
                           e.tensor_mul(dust, dst_, xct), drain=True)
                du_rd[sg] = vd
                P.wait("gpsimd", "DVE", vd)
                P.dma(
                    "gpsimd", "sdu", sDu.ap()[m * 128:(m + 1) * 128,
                                              tt0:tt0 + c.TA], dust)

    def mamba_scan(b, dirn):
        for s in ("ssz", "sxd", "sdl", "sdu", "sbc"):
            P.dwait("sync", s)
        delS = af32(MB_DEL, u_mb)
        delv = delS.rearrange("p (ct t) -> p ct t", ct=KC)
        duS = af32(MB_DU, u_mb)
        duv = duS.rearrange("p (ct t) -> p ct t", ct=KC)
        szS = af32(MB_SZ, u_mb)
        xdS = af32(MB_XD, u_mb)
        brS = af32(MB_BR, u_br)
        crS = af32(MB_CR, u_br)
        daF = af32(MB_DA, u_br)
        ytF = af32(MB_YTF, u_mb)
        ygF = abf16(MB_YGF, u_mb // 2)
        hst = af32(MB_H, KC * c.NS)
        v_scan = {}
        v_red = {}
        g_dbu = {}
        g_mul = {}
        a_da = {}
        a_hc = {}
        prev_vd = None
        NCT = NCB * KC

        def S_of(g):
            return af32(MB_S + (g % 2) * u_br, u_br)

        def db_of(g):
            return af32(MB_DB + (g % 2) * u_br, u_br)

        for cb_ in range(NCB):
            tt0 = cb_ * c.TB
            if cb_ > 0:
                g0 = (cb_ - 1) * KC
                P.wait("sync", "ACT", a_da.get(g0 + KC - 1))
                P.wait("sync", "DVE", prev_vd)
            P.dma("sync", "mbl", delv,
                  sDel.ap().rearrange("(ct p) t -> p ct t", p=128)
                      [:, :, tt0:tt0 + c.TB])
            P.dma("sync", "mbl", duv,
                  sDu.ap().rearrange("(ct p) t -> p ct t", p=128)
                      [:, :, tt0:tt0 + c.TB])
            P.dma("sync", "mbl", szS.rearrange("p (ct t) -> p ct t", ct=KC),
                  sSz.ap().rearrange("(ct p) t -> p ct t", p=128)
                      [:, :, tt0:tt0 + c.TB])
            P.dma("sync", "mbl", xdS.rearrange("p (ct t) -> p ct t", ct=KC),
                  sXd.ap().rearrange("(ct p) t -> p ct t", p=128)
                      [:, :, tt0:tt0 + c.TB])
            P.dma("sync", "mbl",
                  brS.rearrange("p (n t) -> p n t", n=c.NS),
                  sB.ap()[:, tt0:tt0 + c.TB].unsqueeze(0)
                     .broadcast_to([128, c.NS, c.TB]))
            P.dma("sync", "mbl",
                  crS.rearrange("p (n t) -> p n t", n=c.NS),
                  sC.ap()[:, tt0:tt0 + c.TB].unsqueeze(0)
                     .broadcast_to([128, c.NS, c.TB]))
            P.dwait("gpsimd", "mbl")
            P.dwait("scalar", "mbl")
            P.dwait("vector", "mbl")
            if cb_ == 0:
                P.dwait("scalar", "syg")
            for ct in range(KC):
                g = cb_ * KC + ct
                delT = delv[:, ct]
                duT = duv[:, ct]
                S_ = S_of(g)
                Sv = S_.rearrange("p (n t) -> p n t", n=c.NS)
                dbF = db_of(g)
                P.wait("scalar", "DVE", v_scan.get(g - 1))
                va = None
                for n in range(c.NS):
                    acol = A_sb[:, dirn * KC * c.NS + ct * c.NS + n:
                                dirn * KC * c.NS + ct * c.NS + n + 1]
                    va = P.run("scalar", lambda e, delT=delT, n=n, acol=acol:
                               e.activation(
                                   daF[:, n * c.TB:(n + 1) * c.TB], delT,
                                   AF.Exp, scale=acol),
                               inc=(n == c.NS - 1))
                a_da[g] = va
                vb = P.run("vector", lambda e, duT=duT, dbF=dbF:
                           e.tensor_tensor(
                               dbF.rearrange("p (n t) -> p n t", n=c.NS),
                               duT[:, None, :].broadcast_to(
                                   [128, c.NS, c.TB]),
                               brS.rearrange("p (n t) -> p n t", n=c.NS),
                               ALU.mult), inc=True)
                P.wait("scalar", "DVE", vb)
                P.wait("scalar", "ACT", P.sems["ACT"].n)
                ve = P.run("scalar", lambda e: e.nop(), inc=True)
                P.wait("vector", "ACT", ve)
                P.wait("vector", "ACT", a_hc.get(g - 2))
                for n in range(c.NS):
                    init = 0.0 if cb_ == 0 else hst[:, ct * c.NS + n:
                                                    ct * c.NS + n + 1]
                    v_scan[g] = P.run(
                        "vector", lambda e,
                        so=S_[:, n * c.TB:(n + 1) * c.TB],
                        da_=daF[:, n * c.TB:(n + 1) * c.TB],
                        db_=dbF[:, n * c.TB:(n + 1) * c.TB],
                        init=init: e.tensor_tensor_scan(
                            so, da_, db_, init, ALU.mult, ALU.add),
                        inc=(n == c.NS - 1)) or v_scan.get(g)
                P.wait("scalar", "DVE", v_scan[g])
                a_hc[g] = P.run("scalar", lambda e, Sv=Sv, ct=ct: e.copy(
                    hst[:, ct * c.NS:(ct + 1) * c.NS],
                    Sv[:, :, c.TB - 1]), inc=True)
                P.wait("vector", "ACT", a_hc[g])
                vm = P.run("vector", lambda e, S_=S_, dbF=dbF:
                           e.tensor_mul(dbF, S_, crS), inc=True)
                P.wait("scalar", "DVE", vm)
                P.wait("scalar", "ACT", P.sems["ACT"].n)
                ve2 = P.run("scalar", lambda e: e.nop(), inc=True)
                P.wait("vector", "ACT", ve2)
                v_red[g] = P.run("vector", lambda e, dbF=dbF, ct=ct:
                                 e.reduce_sum(
                                     ytF.rearrange("p (ct t) -> p ct t",
                                                   ct=KC)[:, ct],
                                     dbF.rearrange("p (n t) -> p t n",
                                                   t=c.TB),
                                     axis=AX.X), inc=True)
                g_mul[g] = v_red[g]
            P.run("vector", lambda e: e.drain())
            P.dwait("vector", "syg")
            P.wait("vector", "ACT", a_hc[cb_ * KC + KC - 1])
            P.run("vector", lambda e: e.tensor_add(ytF, ytF, xdS))
            P.run("vector", lambda e: e.drain())
            vd = P.run("vector", lambda e: e.tensor_mul(ygF, ytF, szS),
                       drain=True)
            prev_vd = vd
            P.wait("gpsimd", "DVE", vd)
            P.dma("gpsimd", "syg",
                  yg_d[dirn].ap().rearrange("(ct p) t -> p ct t", p=128)
                      [:, :, tt0:tt0 + c.TB],
                  ygF.rearrange("p (ct t) -> p ct t", ct=KC))
            if cb_ + 1 < NCB:
                P.dwait("scalar", "syg")

    def mamba_out(b):
        P.dwait("sync", "syg")
        ow = abf16(MC_OW, u_ow)
        owv = ow.rearrange("p (d k md) -> p d k md", d=2, k=KC)
        ygm = abf16(MC_YG, u_ygm)
        ygmv = ygm.rearrange("p (d ct t) -> p d ct t", d=2, ct=KC)
        for dirn in range(2):
            for k in range(KC):
                P.dma("sync", "owl", owv[:, dirn, k],
                      boutw[dirn].ap()[k * 128:(k + 1) * 128, :])
        o_ev = [None, None]
        x2_st = [None, None]
        x1_rd = [None, None]
        ym_last = None
        for m in range(LT):
            P.wait("sync", "PE", ym_last)
            for dirn in range(2):
                P.dma("sync", "ygl", ygmv[:, dirn],
                      yg_d[dirn].ap().rearrange("(ct p) t -> p ct t", p=128)
                      [:, :, m * 128:(m + 1) * 128])
            x1sl = af32(MC_X1 + (m % 2) * KD * 128, KD * 128)
            x1s = "x1a" if m % 2 == 0 else "x1b"
            P.wait("sync", "DVE", x1_rd[m % 2])
            P.dma("sync", x1s, x1sl,
                  x1_d.ap()[b * c.L + m * 128:b * c.L + (m + 1) * 128, :])
            P.dwait("tensor", "ygl")
            if m == 0:
                P.dwait("tensor", "owl")
            P.dwait("vector", "sx2")
            x2sl = af32(MC_X2 + (m % 2) * KD * 128, KD * 128)
            for f in range(NF):
                so = f % 2
                po = pb[so]
                P.wait("tensor", "DVE", o_ev[so])

                def mmo(e, f=f, po=po):
                    first = True
                    ins = None
                    for dirn in range(2):
                        for k in range(KC):
                            ins = e.matmul(
                                po[:, 0:FD],
                                ygmv[:, dirn, k],
                                owv[:, dirn, k, f * FD:(f + 1) * FD],
                                start=first,
                                stop=(dirn == 1 and k == KC - 1))
                            first = False
                    return ins
                vp = P.run("tensor", mmo, inc=True)
                ym_last = vp
                P.wait("vector", "PE", vp)
                P.dwait("vector", x1s)
                vd = P.run("vector", lambda e, f=f, po=po, x1sl=x1sl,
                           x2sl=x2sl: e.tensor_add(
                               x2sl[:, f * FD:(f + 1) * FD],
                               x1sl[:, f * FD:(f + 1) * FD], po[:, 0:FD]),
                           inc=True)
                o_ev[so] = vd
            vfin = P.drain_inc("vector")
            x1_rd[m % 2] = vfin
            P.wait("gpsimd", "DVE", vfin)
            P.dma(
                "gpsimd", "sx2",
                x2_d.ap()[b * c.L + m * 128:b * c.L + (m + 1) * 128, :],
                x2sl)

    prep()
    P.barrier()
    ffn_call(0)
    P.barrier()
    for b in range(c.BL):
        mamba_batch(b)
    P.barrier()
    ffn_call(1)
    for s in ("sfo", "sx2", "syg"):
        P.dwait("gpsimd", s)

    P.emit_all()
    ctx.close()
    return nc


def prep_inputs(cfg, inputs, ncores=NCORES):
    c = cfg
    KC = c.CI // 128
    g = lambda k: np.asarray(inputs[k])
    x = g("L0_attn_x").reshape(ncores, c.BL * c.L, c.D)
    T = lambda a: np.ascontiguousarray(np.swapaxes(a, -1, -2))
    common = {
        "fwgT": T(g("L0_ffn_wg")),
        "fwuT": T(g("L0_ffn_wu")),
        "fwdT": T(g("L0_ffn_wd")),
        "fgT": T(g("L0_ffn_gate_w")),
        "ident": np.eye(128, dtype=ml_dtypes.bfloat16),
        "antid": np.ascontiguousarray(
            np.flipud(np.eye(128)).astype(ml_dtypes.bfloat16)),
    }
    for sfx in ("1", "2"):
        common[f"inwT{sfx}"] = T(g(f"L0_attn_in_w{sfx}"))
        common[f"outwT{sfx}"] = T(g(f"L0_attn_out_w{sfx}"))
        common[f"xpwT{sfx}"] = T(g(f"L0_attn_xp_w{sfx}"))
        common[f"dtpwT{sfx}"] = T(g(f"L0_attn_dtp_w{sfx}"))
        common[f"convw{sfx}"] = np.ascontiguousarray(
            g(f"L0_attn_conv_w{sfx}").reshape(c.CI, 4)
            .reshape(KC, 128, 4).transpose(1, 0, 2).reshape(128, KC * 4))
        common[f"convb{sfx}"] = np.ascontiguousarray(
            g(f"L0_attn_conv_b{sfx}").reshape(KC, 128).T)
        common[f"dtpb{sfx}"] = np.ascontiguousarray(
            g(f"L0_attn_dtp_b{sfx}").reshape(KC, 128).T)
        common[f"dvec{sfx}"] = np.ascontiguousarray(
            g(f"L0_attn_Dvec{sfx}").reshape(KC, 128).T)
        common[f"alog{sfx}"] = np.ascontiguousarray(
            g(f"L0_attn_A_log{sfx}").reshape(KC, 128, c.NS)
            .transpose(1, 0, 2).reshape(128, KC * c.NS))
    in_maps = []
    for i in range(ncores):
        m = dict(common)
        m["x"] = np.ascontiguousarray(x[i])
        in_maps.append(m)
    return in_maps


def _install_fast_compile():
    try:
        import concourse.bass_utils as bu
        if getattr(bu, "_fast_compile_installed", False):
            return
        from pathlib import Path as _Path

        def bir_verify_and_optimise(tmpdir, inp="bir.json", outp="file.neff",
                                    arch=None, *, dve_root=None):
            cmd = [
                bu.get_walrus_driver(),
                "--pass",
                ",".join([
                    "runtime_memory_reservation",
                    "lower_act",
                    "lower_dve",
                    "lower_ap_offset",
                    "codegen",
                    "neff_packager",
                ]),
                "-i", inp,
                "--neff-output-filename", outp,
                "--enable-birsim=true",
                "--mem-mode=physical",
                "--policy=0",
                "--enable-ldw-opt=false",
                "--assign-static-dmas-to-sp=false",
                "--dram-page-size=256",
                "--enable-neff-debug-info=true",
                "--jobs", "8",
                *bu.get_walrus_args(
                    bu.get_bir_arch(tmpdir, inp) if arch is None else arch,
                    tmpdir, dve_root=dve_root),
            ]
            result = bu.run_command(cmd, cwd=tmpdir)
            if result is not None:
                (_Path(tmpdir) / "log.txt").write_text(result.stdout)
            return str(_Path(tmpdir) / outp)

        bu.bir_verify_and_optimise = bir_verify_and_optimise
        bu._fast_compile_installed = True
    except Exception:
        pass


def kernel(**inputs):
    _install_fast_compile()
    cfg = FULL
    in_maps = prep_inputs(cfg, inputs)
    nc = build_program(cfg)
    res = run_bass_kernel_spmd(nc, in_maps, core_ids=list(range(NCORES)))
    out = np.concatenate([r["out"] for r in res.results], axis=0)
    return out.reshape(32, cfg.L, cfg.D)
